# version 41
# baseline (speedup 1.0000x reference)
import numpy as np
import ml_dtypes
from contextlib import ExitStack

import concourse.bass as bass
import concourse.mybir as mybir
from concourse.bass_utils import run_bass_kernel_spmd

F32 = mybir.dt.float32
BF16 = mybir.dt.bfloat16
AF = mybir.ActivationFunctionType
OP = mybir.AluOpType
BF = ml_dtypes.bfloat16

S = 2048
D = 1024
H = 16
KVH = 4
HD = 64
LAT = 256
WIN = 256
DFF = 4096
NSEQ = 2
NCH = 8
NTC = 16
C_EXP_SCALE = 0.125
C_EXP_BIAS = -8.0
C_INV_D = 1.0 / 1024.0
C_INV_HD = 0.015625
C_EPS_RMS = 1e-6
C_EPS_LN = 1e-5
C_NHALF = -0.5

ENGINES = ("sync", "tensor", "vector", "scalar", "gpsimd")


def _jax_rope_tables(base):
    import jax
    import jax.numpy as jnp
    with jax.default_device(jax.devices("cpu")[0]):
        inv_freq = 1.0 / base ** (jnp.arange(0, HD, 2, dtype=jnp.float32) / HD)
        freqs = jnp.arange(S, dtype=jnp.float32)[:, None] * inv_freq[None, :]
        cos, sin = jnp.cos(freqs), jnp.sin(freqs)
        return (np.asarray(cos, np.float32).T.copy(),
                np.asarray(sin, np.float32).T.copy())


def host_constants():
    c = {}
    for base, name in ((1e6, "ropeg"), (1e4, "ropel")):
        cos, sin = _jax_rope_tables(base)
        Ct = np.zeros((128, 2, S), np.float32)
        for p in range(128):
            hd = p % 64
            j = hd // 2
            Ct[p, 0] = cos[j]
            Ct[p, 1] = -sin[j] if hd % 2 == 0 else sin[j]
        c[name] = Ct
    perm = np.zeros((128, 128), BF)
    for m in range(128):
        perm[m ^ 1, m] = 1.0
    c["perm"] = perm
    c["iden"] = np.eye(128, dtype=np.float32)
    i = np.arange(128)
    mdiag = (i[None, :] >= i[:, None]).astype(BF)
    mlow = (i[None, :] < i[:, None]).astype(BF)
    c["masks"] = np.ascontiguousarray(np.stack([mdiag, mlow], axis=1))
    hsel = np.zeros((128, 8, 16), BF)
    for ch in range(8):
        hsel[0:64, ch, 2 * ch] = 1.0
        hsel[64:128, ch, 2 * ch + 1] = 1.0
    c["hsel"] = hsel
    sel16 = np.zeros((16, 8, 128), BF)
    for ch in range(8):
        sel16[2 * ch, ch, 0:64] = 1.0
        sel16[2 * ch + 1, ch, 64:128] = 1.0
    c["sel16"] = sel16
    sel2 = np.zeros((64, 128), BF)
    sel2[0, 0:64] = 1.0
    sel2[32, 64:128] = 1.0
    c["sel2"] = sel2
    c["onesM"] = np.ones((128, 32), BF)
    c["ones128"] = np.ones((128, 1), BF)
    c["onesk1b"] = np.ones((1, 128), BF)
    c["zrow"] = np.zeros((1, 576), BF)
    cp = np.zeros((128, 16), np.float32)
    vals = [C_EXP_SCALE, C_EXP_BIAS, C_INV_D, C_EPS_RMS, C_NHALF, C_INV_HD,
            C_EPS_LN, 1.0, 0.0, 0.5, -1.0]
    for j, v in enumerate(vals):
        cp[:, j] = v
    c["cpool"] = cp
    c["_cvals"] = vals
    return c


class KB:
    def __init__(self, nc, stack):
        self.nc = nc
        self.pend = []
        self.cnt = {}
        self.hwm = {}
        self.sems = {}
        self.bgate = [dict() for _ in range(8)]
        for name in ("LDC", "LS0", "LS1", "LW0", "LW1", "LX0", "LX1",
                     "LSF", "W2L0", "W2L1", "SOU", "SRES", "SX3", "KD",
                     "PE", "ACT", "DVE", "POOL", "SY"):
            self.sems[name] = stack.enter_context(nc.semaphore(name))
            self.cnt[name] = 0

    def op(self, eng, fn, waits=(), inc=None, amt=1):
        merged = {}
        for t in waits:
            if t is None:
                continue
            s, v = t
            if merged.get(s, 0) < v:
                merged[s] = v
        w2 = []
        for s, v in merged.items():
            key = (eng, s)
            if self.hwm.get(key, 0) < v:
                self.hwm[key] = v
                w2.append((s, v))
        tok = None
        if inc:
            self.cnt[inc] += amt
            tok = (inc, self.cnt[inc])
        self.pend.append((eng, fn, w2, (inc, amt) if inc else None))
        return tok

    def wait(self, eng, *toks):
        for t in toks:
            if t is None:
                continue
            self.op(eng, None, waits=(t,))

    def dma(self, eng, sem, out, in_, waits=()):
        return self.op(eng, lambda e, o=out, i=in_: e.dma_start(out=o, in_=i),
                       waits, inc=sem, amt=16)

    def bank_of(self, col):
        return col // 512

    def gates(self, c0, c1):
        out = {}
        for b in range(self.bank_of(c0), self.bank_of(c1 - 1) + 1):
            for s, v in self.bgate[b].items():
                if out.get(s, 0) < v:
                    out[s] = v
        return tuple(out.items())

    def read_psum(self, c0, c1, tok):
        if tok is None:
            return
        s, v = tok
        for b in range(self.bank_of(c0), self.bank_of(c1 - 1) + 1):
            if self.bgate[b].get(s, 0) < v:
                self.bgate[b][s] = v

    def barrier(self):
        toks = []
        ctr = {"sync": "SY", "tensor": "PE", "vector": "DVE",
               "scalar": "ACT", "gpsimd": "POOL"}
        for eng in ENGINES:
            toks.append(self.op(eng, lambda e: e.nop(), inc=ctr[eng]))
        for eng in ENGINES:
            self.wait(eng, *toks)
        for b in range(8):
            for t in toks:
                s, v = t
                if self.bgate[b].get(s, 0) < v:
                    self.bgate[b][s] = v

    def flush(self):
        pend = self.pend
        self.pend = []
        sems = self.sems
        with self.nc.Block() as blk:
            for eng in ENGINES:
                ops = [p for p in pend if p[0] == eng]
                if not ops:
                    continue

                def body(e, ops=ops):
                    for (_, fn, ws, inc) in ops:
                        for (s, v) in ws:
                            e.wait_ge(sems[s], v)
                        if fn is None:
                            continue
                        inst = fn(e)
                        if inc is not None:
                            inst.then_inc(sems[inc[0]], inc[1])
                getattr(blk, eng)(body)


def build_core_program(debug=False):
    nc = bass.Bass("TRN2", target_bir_lowering=False,
                   detect_race_conditions=False)
    hc = host_constants()

    x_in = nc.dram_tensor("x_in", [NSEQ, S, D], F32, kind="ExternalInput").ap()
    y_out = nc.dram_tensor("y_out", [NSEQ, S, D], F32, kind="ExternalOutput").ap()
    wd = {}
    for nm, shp in (("gwq", [8, 128, 8, 128]), ("gkvd", [128, 8, LAT]),
                    ("gkup", [128, 2, LAT]), ("gvup", [128, 2, LAT]),
                    ("gwo", [8, 128, 8, 128]), ("lwq", [8, 128, 8, 128]),
                    ("lwk", [128, 8, LAT]), ("lwv", [128, 8, LAT]),
                    ("lwo", [8, 128, 8, 128]), ("w1c", [32, 128, 8, 128]),
                    ("w2c", [128, 32, 1024])):
        wd[nm] = nc.dram_tensor(nm, shp, F32, kind="ExternalInput").ap()
    n1t_d = nc.dram_tensor("n1t", [128, 8], F32, kind="ExternalInput").ap()
    n2t_d = nc.dram_tensor("n2t", [128, 8], F32, kind="ExternalInput").ap()
    b1t_d = nc.dram_tensor("b1t", [128, 32], F32, kind="ExternalInput").ap()
    b2t_d = nc.dram_tensor("b2t", [128, 8], F32, kind="ExternalInput").ap()
    cd = {}
    for nm in ("ropeg", "ropel", "perm", "iden", "masks", "hsel", "sel16",
               "sel2", "onesM", "ones128", "onesk1b", "zrow", "cpool"):
        arr = hc[nm]
        dt = BF16 if arr.dtype == BF else F32
        cd[nm] = nc.dram_tensor(nm, list(arr.shape), dt, kind="ExternalInput").ap()

    if debug:
        dbg_t = nc.dram_tensor("dbg_t", [NCH, 128, S], F32, kind="ExternalOutput").ap()
        dbg_x3 = nc.dram_tensor("dbg_x3", [NCH, 128, S], F32, kind="ExternalOutput").ap()
        dbg_xn = nc.dram_tensor("dbg_xn", [NCH, 128, S], F32, kind="ExternalOutput").ap()
        dbg_rows = nc.dram_tensor("dbg_rows", [2, S], F32, kind="ExternalOutput").ap()
        dbg_q = nc.dram_tensor("dbg_q", [128, S], BF16, kind="ExternalOutput").ap()
        dbg_k = nc.dram_tensor("dbg_k", [128, KVH, S], BF16, kind="ExternalOutput").ap()
    scr0 = nc.dram_tensor("scr0", [NSEQ, NCH, 128, S], F32).ap()
    scrA = nc.dram_tensor("scrA", [NCH, 128, S], F32).ap()
    scrB = nc.dram_tensor("scrB", [NCH, 128, S], F32).ap()

    st = ExitStack()
    with st:
        kb = KB(nc, st)
        T = lambda *a: st.enter_context(nc.sbuf_tensor(*a))
        xnT = T("xnT", [128, NCH, S], BF16)
        r1 = T("r1", [1, 1024], F32)
        r2 = T("r2", [1, 1024], F32)
        r3 = T("r3", [1, 1024], F32)
        rh = T("rh", [1, 1024], BF16)
        rl = T("rl", [1, 1024], BF16)
        wstage = T("wstage", [128, S], F32)
        outstage = T("outstage", [128, 1024], F32)
        sqb = T("sqb", [128, 2 * S], BF16)
        cpool = T("cpool_s", [128, 16], F32)
        n1t = T("n1t_s", [128, 8], F32)
        n2t = T("n2t_s", [128, 8], F32)
        b1t = T("b1t_s", [128, 32], F32)
        b2t = T("b2t_s", [128, 8], F32)
        perm = T("perm_s", [128, 128], BF16)
        iden = T("iden_s", [128, 128], F32)
        masks = T("masks_s", [128, 2, 128], BF16)
        hsel = T("hsel_s", [128, 8, 16], BF16)
        sel16 = T("sel16_s", [16, 8, 128], BF16)
        sel2 = T("sel2_s", [64, 128], BF16)
        onesM = T("onesM_s", [128, 32], BF16)
        ones128 = T("ones128_s", [128, 1], BF16)
        onesk1b = T("onesk1b_s", [1, 128], BF16)
        zrow = T("zrow_s", [1, 576], BF16)
        A = st.enter_context(nc.psum_tensor("arena", [128, 4096], F32))

        for j, v in enumerate(hc["_cvals"]):
            nc.const_aps.register(F32, float(v), cpool[:, j:j + 1])

        def mm(out_cols, out_ap, lhsT, rhs, start, stop, waits=(), skip=False,
               tile_position=None):
            c0, c1 = out_cols
            w = tuple(waits) + kb.gates(c0, c1)
            return kb.op("tensor",
                         lambda e, o=out_ap, l=lhsT, r=rhs, st_=start, sp=stop,
                                tp=tile_position, sk=skip:
                         e.matmul(o, l, r, start=st_, stop=sp,
                                  tile_position=tp, skip_group_check=sk),
                         w, inc="PE")

        def trans(out_cols, out_ap, in_ap, waits=()):
            c0, c1 = out_cols
            w = tuple(waits) + kb.gates(c0, c1)
            return kb.op("tensor",
                         lambda e, o=out_ap, i=in_ap: e.transpose(o, i, iden[...]),
                         w, inc="PE")

        def dve(fn, waits=(), psum_read=None):
            t = kb.op("vector", fn, waits, inc="DVE")
            if psum_read:
                kb.read_psum(psum_read[0], psum_read[1], t)
            return t

        def act(fn, waits=(), psum_read=None):
            t = kb.op("scalar", fn, waits, inc="ACT")
            if psum_read:
                kb.read_psum(psum_read[0], psum_read[1], t)
            return t

        def pool(fn, waits=()):
            return kb.op("gpsimd", fn, waits, inc="POOL")

        ld = []
        for nm, dst in (("perm", perm), ("iden", iden), ("masks", masks),
                        ("hsel", hsel), ("sel16", sel16), ("sel2", sel2),
                        ("ones128", ones128), ("onesM", onesM),
                        ("onesk1b", onesk1b),
                        ("zrow", zrow), ("cpool", cpool)):
            ld.append(kb.dma("sync", "LDC", dst[...], cd[nm]))
        ld.append(kb.dma("sync", "LDC", n1t[...], n1t_d))
        ld.append(kb.dma("sync", "LDC", n2t[...], n2t_d))
        ld.append(kb.dma("sync", "LDC", b1t[...], b1t_d))
        ld.append(kb.dma("sync", "LDC", b2t[...], b2t_d))
        for eng in ("vector", "scalar", "tensor", "gpsimd"):
            kb.wait(eng, ld[-1])

        zl64 = zrow[0:1, 0:64]
        zl1 = zrow[0:1, 0:1]
        zl32 = zrow[0:1, 0:32]
        zr512 = zrow[0:1, 64:576]

        def cast_weight(dram, dst, nkc, ncols, fold=None, waits=(),
                        eng=None):
            eng = eng or dve
            t_l = kb.dma("sync", "LW0", wstage[:, 0:nkc * ncols]
                         .rearrange("p (kc n) -> p kc n", kc=nkc), dram,
                         waits=waits)
            toks = []
            for j in range(nkc):
                stg = wstage[:, j * ncols:(j + 1) * ncols]
                dst_ap = dst[:, j, 0:ncols]
                if fold is not None:
                    toks.append(eng(lambda e, d=dst_ap, s=stg,
                                    f=fold[:, j:j + 1]:
                                    e.tensor_scalar(d, s, f, None, OP.mult),
                                    waits=(t_l,)))
                else:
                    toks.append(eng(lambda e, d=dst_ap, s=stg:
                                    e.tensor_copy(d, s), waits=(t_l,)))
            return toks

        strmF_guard = [None, None]
        for s in range(NSEQ):
            ast = ExitStack()
            with ast:
                Ta = lambda n, *a: ast.enter_context(nc.sbuf_tensor(n + "_%d" % s, *a))
                k2 = Ta("k2", [128, KVH, S], BF16)
                v_all = Ta("v_all", [128, NTC, KVH, HD], BF16)
                qbf = Ta("qbf", [128, S], BF16)
                qrope = Ta("qrope", [128, S], F32)
                qfin = Ta("qfin", [128, S], BF16)
                pab = Ta("pab", [128, 6, 512], BF16)
                t1buf = pab[:, 0:4, :].rearrange("p a b -> p (a b)")
                OT = Ta("OT", [128, NCH, S], BF16)
                wqb = Ta("wqb", [128, 2, NCH, 128], BF16)
                wob = Ta("wob", [128, 2, NCH, 128], BF16)
                wkvb = Ta("wkvb", [128, NCH, LAT], BF16)
                wvb = Ta("wvb", [128, NCH, LAT], BF16)
                ropeC = Ta("ropeC", [128, S], BF16)
                ropeS = sqb[:, 2048:4096]
                rstd16f = Ta("rstd16f", [16, 1024], F32)
                rstd16b = Ta("rstd16b", [16, S], BF16)
                rcp = Ta("rcp", [64, 512], BF16)
                strm = [Ta("strm0", [128, S], F32), Ta("strm1", [128, S], F32)]
                latb = Ta("latb", [128, 2, S], BF16)
                lat = [latb[:, 0, :], latb[:, 1, :]]

                kb.barrier()

                ttr_last = [None, None]
                tcp_prev = [None, None]
                tsq_prev = None
                tst_prev = None
                tmm_prev = [None, None]
                st_toks = []
                for tc in range(NTC):
                    par = tc % 2
                    t_l = kb.dma("sync", "LS%d" % par, strm[par][:, 0:1024],
                                 x_in[s, tc * 128:(tc + 1) * 128, :],
                                 waits=(ttr_last[par],))
                    ttr = []
                    for k in range(NCH):
                        cc = par * 1024 + k * 128
                        ttr.append(trans((cc, cc + 128), A[:, cc:cc + 128],
                                         strm[par][:, k * 128:k * 128 + 128],
                                         waits=(t_l,)))
                    ttr_last[par] = ttr[-1]
                    t_cp = dve(lambda e, p=par: e.tensor_copy(
                        outstage[...], A[:, p * 1024:p * 1024 + 1024]),
                        waits=tuple(ttr) + (tsq_prev, tst_prev),
                        psum_read=(par * 1024, par * 1024 + 1024))
                    t_sq = act(lambda e, p=par: e.activation(
                        sqb[:, p * 1024:p * 1024 + 1024], outstage[...],
                        AF.Square), waits=(t_cp, tmm_prev[par]))
                    tsq_prev = t_sq
                    tm = []
                    for k in range(NCH):
                        tm.append(mm((2048 + tc * 128, 2048 + tc * 128 + 128),
                                     A[0:1, 2048 + tc * 128:2048 + tc * 128 + 128],
                                     ones128[...],
                                     sqb[:, par * 1024 + k * 128:
                                         par * 1024 + k * 128 + 128],
                                     start=(k == 0), stop=(k == NCH - 1),
                                     waits=(t_sq,)))
                    tmm_prev[par] = tm[-1]
                    t_st = kb.dma("scalar", "SOU",
                                  scr0[s][:, :, tc * 128:(tc + 1) * 128]
                                  .transpose([1, 0, 2]),
                                  outstage[...].rearrange("p (k t) -> p k t", k=8),
                                  waits=(t_cp,))
                    tst_prev = t_st
                    st_toks.append(t_st)
                x0_done = st_toks[-1]

                t_bcast = []
                for hh in range(2):
                    t_ln = act(lambda e, hh=hh: e.activation(
                        r1[...], A[0:1, 2048 + hh * 1024:3072 + hh * 1024],
                        AF.Ln, bias=C_EPS_RMS, scale=C_INV_D),
                        waits=(tmm_prev[0], tmm_prev[1]) + tuple(t_bcast),
                        psum_read=(2048 + hh * 1024, 3072 + hh * 1024))
                    t_rms = act(lambda e: e.activation(
                        r2[...], r1[...], AF.Exp, scale=C_NHALF),
                        waits=(t_ln,))
                    t_h = dve(lambda e: e.tensor_copy(rh[...], r2[...]),
                              waits=(t_rms,) + tuple(t_bcast))
                    t_lo = dve(lambda e: e.tensor_tensor(
                        rl[...], r2[...], rh[...], OP.subtract),
                        waits=(t_h,))
                    for p in range(2):
                        cc = hh * 1024 + p * 512
                        mm((cc, cc + 512), A[:, cc:cc + 512], onesk1b[...],
                           rh[:, p * 512:p * 512 + 512],
                           start=True, stop=False, waits=(t_lo,))
                        t_bcast.append(mm((cc, cc + 512), A[:, cc:cc + 512],
                                          onesk1b[...],
                                          rl[:, p * 512:p * 512 + 512],
                                          start=False, stop=True,
                                          waits=(t_lo,)))
                t_xn = [None] * NCH
                prevl = [None, None]
                for k in range(NCH):
                    par = k % 2
                    t_l = kb.dma("sync", "LS%d" % par, strm[par][...],
                                 scr0[s][k], waits=(prevl[par], x0_done))
                    t_xn[k] = dve(lambda e, kk=k, p=par: e.tensor_tensor(
                        xnT[:, kk, :], strm[p][...], A[:, 0:2048], OP.mult),
                        waits=(t_l,) + tuple(t_bcast), psum_read=(0, 2048))
                    prevl[par] = t_xn[k]
                t_xn_all = tuple(t_xn)

                for phase in range(2):
                    is_g = (phase == 0)
                    if not is_g:
                        kb.barrier()
                    foldv = n1t if is_g else n2t

                    tk_kv = cast_weight(wd["gkvd" if is_g else "lwk"], wkvb,
                                        8, LAT, fold=foldv)
                    if is_g:
                        tk_up = cast_weight(wd["gkup"], wvb, 2, LAT,
                                            waits=(tk_kv[-1],))
                        t_lv = kb.dma("sync", "LW0", wstage[:, 0:512]
                                      .rearrange("p (kc n) -> p kc n", kc=2),
                                      wd["gvup"], waits=(tk_up[-1],))
                        tk_vp = [dve(lambda e, j=j: e.tensor_copy(
                            wvb[:, 2 + j, :], wstage[:, j * 256:j * 256 + 256]),
                            waits=(t_lv,)) for j in range(2)]
                    else:
                        tk_vp = cast_weight(wd["lwv"], wvb, 8, LAT, fold=foldv,
                                            waits=(tk_kv[-1],))
                    rope_d = cd["ropeg" if is_g else "ropel"]
                    t_lr = kb.dma("sync", "LW0", wstage[...], rope_d[:, 0, :],
                                  waits=(tk_vp[-1],))
                    t_rc = dve(lambda e: e.tensor_copy(ropeC[...], wstage[...]),
                               waits=(t_lr,))
                    t_lr2 = kb.dma("sync", "LW0", wstage[...], rope_d[:, 1, :],
                                   waits=(t_rc,))
                    t_rs = dve(lambda e: e.tensor_copy(ropeS, wstage[...]),
                               waits=(t_lr2,))
                    wstage_free = t_rs

                    lat_toks = ()
                    if is_g:
                        lt = []
                        for m in range(2):
                            tmm = []
                            for k in range(NCH):
                                for p in range(4):
                                    tmm.append(mm(
                                        (p * 512, p * 512 + 512),
                                        A[:, p * 512:p * 512 + 512],
                                        wkvb[:, k, m * 128:m * 128 + 128],
                                        xnT[:, k, p * 512:p * 512 + 512],
                                        start=(k == 0), stop=(k == NCH - 1),
                                        waits=(tk_kv[k],) + t_xn_all))
                            lt.append(dve(lambda e, m=m: e.tensor_copy(
                                lat[m], A[:, 0:2048]), waits=tuple(tmm),
                                psum_read=(0, 2048)))
                        lat_toks = tuple(lt)

                    def rope_qknorm(chunk_idx, t_evac, dst_fn):
                        t_perm = []
                        for p in range(4):
                            t_perm.append(mm((p * 512, p * 512 + 512),
                                             A[:, p * 512:p * 512 + 512],
                                             perm[...],
                                             qbf[:, p * 512:p * 512 + 512],
                                             start=True, stop=True,
                                             waits=(t_evac, t_rc, t_rs)))
                        t_t1 = dve(lambda e: e.tensor_tensor(
                            t1buf, qbf[...], ropeC[...], OP.mult),
                            waits=(t_evac, t_rc))
                        t_r1 = dve(lambda e: e.tensor_tensor(
                            qrope[...], A[:, 0:2048], ropeS, OP.mult),
                            waits=tuple(t_perm) + (t_rs,),
                            psum_read=(0, 2048))
                        t_r2 = dve(lambda e: e.tensor_tensor(
                            qrope[...], qrope[...], t1buf, OP.add),
                            waits=(t_t1,))
                        t_sq = act(lambda e: e.activation(
                            sqb[:, 0:2048], qrope[...], AF.Square),
                            waits=(t_r2,))
                        t_ssq = []
                        for p in range(4):
                            t_ssq.append(mm(
                                (2048 + p * 512, 2048 + p * 512 + 512),
                                A[0:16, 2048 + p * 512:2048 + p * 512 + 512],
                                hsel[:, chunk_idx, :],
                                sqb[:, p * 512:p * 512 + 512],
                                start=True, stop=True, waits=(t_sq,)))
                        t_exq = None
                        for hh in range(2):
                            t_lnq = act(lambda e, hh=hh: e.activation(
                                rstd16f[...],
                                A[0:16, 2048 + hh * 1024:3072 + hh * 1024],
                                AF.Ln, bias=C_EPS_RMS, scale=C_INV_HD),
                                waits=tuple(t_ssq) +
                                ((t_exq,) if t_exq else ()),
                                psum_read=(2048 + hh * 1024, 3072 + hh * 1024))
                            t_exq = act(lambda e, hh=hh: e.activation(
                                rstd16b[:, hh * 1024:hh * 1024 + 1024],
                                rstd16f[...], AF.Exp, scale=C_NHALF),
                                waits=(t_lnq,))
                        t_bc = []
                        for p in range(4):
                            t_bc.append(mm((p * 512, p * 512 + 512),
                                           A[:, p * 512:p * 512 + 512],
                                           sel16[:, chunk_idx, :],
                                           rstd16b[:, p * 512:p * 512 + 512],
                                           start=True, stop=True,
                                           waits=(t_exq,)))
                        return dst_fn(t_bc, t_r2)

                    k2_toks = []
                    for m in range(2):
                        tmm = []
                        if is_g:
                            for kc in range(2):
                                for p in range(4):
                                    tmm.append(mm(
                                        (p * 512, p * 512 + 512),
                                        A[:, p * 512:p * 512 + 512],
                                        wvb[:, kc, m * 128:m * 128 + 128],
                                        lat[kc][:, p * 512:p * 512 + 512],
                                        start=(kc == 0), stop=(kc == 1),
                                        waits=lat_toks + (tk_up[kc],)))
                        else:
                            for k in range(NCH):
                                for p in range(4):
                                    tmm.append(mm(
                                        (p * 512, p * 512 + 512),
                                        A[:, p * 512:p * 512 + 512],
                                        wkvb[:, k, m * 128:m * 128 + 128],
                                        xnT[:, k, p * 512:p * 512 + 512],
                                        start=(k == 0), stop=(k == NCH - 1),
                                        waits=(tk_kv[k],) + t_xn_all))
                        t_evac = dve(lambda e: e.tensor_copy(qbf[...],
                                                             A[:, 0:2048]),
                                     waits=tuple(tmm), psum_read=(0, 2048))

                        def k_dst(t_bc, t_r2, m=m):
                            toks = []
                            for hloc in range(2):
                                g = 2 * m + hloc
                                r0 = hloc * 64
                                tt = dve(lambda e, g=g, r0=r0:
                                         e.tensor_tensor(
                                             k2[r0:r0 + 64, g, :],
                                             qrope[r0:r0 + 64, :],
                                             A[r0:r0 + 64, 0:2048],
                                             OP.mult),
                                         waits=tuple(t_bc) + (t_r2,),
                                         psum_read=(0, 2048))
                                d0 = 64 - r0
                                tcp = kb.dma("gpsimd", "KD",
                                             k2[d0:d0 + 64, g, :],
                                             k2[r0:r0 + 64, g, :],
                                             waits=(tt,))
                                toks.append(tcp)
                            return toks
                        k2_toks += rope_qknorm(m, t_evac, k_dst)
                    k2_all = tuple(k2_toks)

                    if debug and s == 0 and is_g:
                        kb.dma("gpsimd", "KD", dbg_k[...], k2[...],
                               waits=tuple(k2_toks))
                    wq_d = wd["gwq" if is_g else "lwq"]
                    wo_d = wd["gwo" if is_g else "lwo"]
                    wq_cast_tok = [wstage_free, wstage_free]
                    ot_toks = []
                    qfin_tok = [[None] * 4 for _ in range(NCH)]
                    tw_by_qc = [None] * NCH
                    rot = [0]
                    rot_mod = [3]

                    def take_slot():
                        w = rot[0] % rot_mod[0]
                        rot[0] += 1
                        return w

                    def wq_cast(qci, foldv=foldv):
                        par = qci % 2
                        t_lw = kb.dma("sync", "LW%d" % par,
                                      wstage[:, 1024 * par:1024 * par + 1024]
                                      .rearrange("p (kc n) -> p kc n", kc=8),
                                      wq_d[qci], waits=(wq_cast_tok[par],))
                        tw = []
                        for j in range(NCH):
                            tw.append(dve(lambda e, j=j, par=par,
                                          fv=foldv[:, j:j + 1]:
                                          e.tensor_scalar(
                                              wqb[:, par, j, :],
                                              wstage[:, 1024 * par + j * 128:
                                                      1024 * par + j * 128
                                                      + 128],
                                              fv, None, OP.mult),
                                          waits=(t_lw,)))
                        tw_by_qc[qci] = tw

                    def mk_chain(qci, pan):
                        lo = pan * 512
                        par = qci % 2
                        cs_ = {}

                        def th_qmm(lo=lo, par=par, qci=qci, cs_=cs_):
                            w = take_slot()
                            cs = w * 1024
                            tmm = []
                            for k in range(NCH):
                                tmm.append(mm(
                                    (cs, cs + 512), A[:, cs:cs + 512],
                                    wqb[:, par, k, :],
                                    xnT[:, k, lo:lo + 512],
                                    start=(k == 0), stop=(k == NCH - 1),
                                    waits=(tw_by_qc[qci][k],) + t_xn_all))
                            wq_cast_tok[par] = tmm[-1]
                            cs_["evac"] = dve(
                                lambda e, cs=cs, lo=lo: e.tensor_copy(
                                    qbf[:, lo:lo + 512], A[:, cs:cs + 512]),
                                waits=(tmm[-1],), psum_read=(cs, cs + 512))

                        def th_perm(lo=lo, cs_=cs_):
                            w = take_slot()
                            cs = w * 1024
                            t_p = mm((cs, cs + 512), A[:, cs:cs + 512],
                                     perm[...], qbf[:, lo:lo + 512],
                                     start=True, stop=True,
                                     waits=(cs_["evac"], t_rc, t_rs))
                            t_t1 = dve(lambda e, lo=lo: e.tensor_tensor(
                                sqb[:, lo:lo + 512], qbf[:, lo:lo + 512],
                                ropeC[:, lo:lo + 512], OP.mult),
                                waits=(cs_["evac"], t_rc))
                            t_r1 = dve(lambda e, cs=cs, lo=lo:
                                       e.tensor_tensor(
                                           qrope[:, lo:lo + 512],
                                           A[:, cs:cs + 512],
                                           ropeS[:, lo:lo + 512], OP.mult),
                                       waits=(t_p, t_rs),
                                       psum_read=(cs, cs + 512))
                            t_r2 = dve(lambda e, lo=lo: e.tensor_tensor(
                                qrope[:, lo:lo + 512], qrope[:, lo:lo + 512],
                                sqb[:, lo:lo + 512], OP.add),
                                waits=(t_t1, t_r1))
                            cs_["sq"] = dve(lambda e, lo=lo: e.tensor_tensor(
                                sqb[:, lo:lo + 512], qrope[:, lo:lo + 512],
                                qrope[:, lo:lo + 512], OP.mult),
                                waits=(t_r2,))

                        def th_ssq(lo=lo, qci=qci, cs_=cs_):
                            w = take_slot()
                            cs = w * 1024
                            t_s = mm((cs, cs + 512), A[0:16, cs:cs + 512],
                                     hsel[:, qci, :], sqb[:, lo:lo + 512],
                                     start=True, stop=True,
                                     waits=(cs_["sq"],))
                            t_ln = act(lambda e, cs=cs: e.activation(
                                rstd16f[:, 0:512], A[0:16, cs:cs + 512],
                                AF.Ln, bias=C_EPS_RMS, scale=C_INV_HD),
                                waits=(t_s,), psum_read=(cs, cs + 512))
                            cs_["exq"] = act(lambda e, lo=lo: e.activation(
                                rstd16b[:, lo:lo + 512], rstd16f[:, 0:512],
                                AF.Exp, scale=C_NHALF), waits=(t_ln,))

                        def th_bc(lo=lo, qci=qci, pan=pan, cs_=cs_):
                            w = take_slot()
                            cs = w * 1024
                            t_b = mm((cs, cs + 512), A[:, cs:cs + 512],
                                     sel16[:, qci, :],
                                     rstd16b[:, lo:lo + 512],
                                     start=True, stop=True,
                                     waits=(cs_["exq"],))
                            qfin_tok[qci][pan] = dve(
                                lambda e, cs=cs, lo=lo: e.tensor_tensor(
                                    qfin[:, lo:lo + 512],
                                    qrope[:, lo:lo + 512],
                                    A[:, cs:cs + 512], OP.mult),
                                waits=(t_b,), psum_read=(cs, cs + 512))

                        return [th_qmm, th_perm, th_ssq, th_bc]

                    wq_cast(0)
                    rot_mod[0] = 2
                    ch0 = []
                    for pan0 in range(4):
                        ch0 += mk_chain(0, pan0)
                    v_toks = []
                    for tc in range(NTC):
                        par = tc % 2
                        c0 = 2048 + par * 512
                        reg = A[:, c0:c0 + 256]
                        tmm = []
                        if is_g:
                            for kc in range(2):
                                tmm.append(mm((c0, c0 + 256), reg,
                                              lat[kc][:, tc * 128:tc * 128 + 128],
                                              wvb[:, 2 + kc, :],
                                              start=(kc == 0), stop=(kc == 1),
                                              waits=lat_toks + (tk_vp[-1],)))
                        else:
                            for k in range(NCH):
                                tmm.append(mm((c0, c0 + 256), reg,
                                              xnT[:, k, tc * 128:tc * 128 + 128],
                                              wvb[:, k, :],
                                              start=(k == 0), stop=(k == NCH - 1),
                                              waits=(tk_vp[k],) + t_xn_all))
                        v_toks.append(dve(lambda e, tc=tc, r=reg: e.tensor_copy(
                            v_all[:, tc, :, :], r), waits=tuple(tmm),
                            psum_read=(c0, c0 + 256)))
                        ch0[tc]()
                    v_all_tok = (v_toks[-1], v_toks[-2])
                    rot_mod[0] = 3

                    for qc in range(NCH):
                        par = qc % 2
                        g = qc // 2
                        if qc + 1 < NCH:
                            wq_cast(qc + 1)
                        if debug and s == 0 and is_g and qc == 0:
                            kb.dma("gpsimd", "KD", dbg_q[...], qfin[...],
                                   waits=tuple(qfin_tok[0]))

                        for pan in range(4):
                            lo = pan * 512
                            hi = lo + 512
                            t_qfin = qfin_tok[qc][pan]
                            ch = (mk_chain(qc + 1, pan)
                                  if qc + 1 < NCH else None)
                            if is_g:
                                chunks = list(range(0, hi // 128))
                            else:
                                chunks = [c for c in range(NTC)
                                          if c * 128 < hi
                                          and (c + 3) * 128 > lo]
                            av_zero = []
                            if not is_g:
                                av_zero.append(mm(
                                    (3072, 3584), A[0:64, 3072:3584],
                                    zl64, zr512, start=True, stop=True,
                                    skip=True, tile_position=(0, 0)))
                                av_zero.append(mm(
                                    (3072, 3584), A[64:128, 3072:3584],
                                    zl64, zr512, start=True, stop=True,
                                    skip=True, tile_position=(0, 64)))
                                av_zero.append(mm(
                                    (3584, 4096), A[0:32, 3584:4096],
                                    zl32, zr512, start=True, stop=True,
                                    skip=True, tile_position=(0, 0)))
                                av_zero.append(mm(
                                    (3584, 4096), A[32:64, 3584:4096],
                                    zl32, zr512, start=True, stop=True,
                                    skip=True, tile_position=(0, 32)))
                            pieces = []
                            for c in chunks:
                                a = max(c * 128, lo)
                                b = hi if is_g else min((c + 3) * 128, hi)
                                pieces.append((c, a, b))
                            av_toks = []
                            rs_toks = []
                            pfree = [None, None, None]
                            pend_av = [None, None, None]

                            piece_slot = {}

                            def emit_scores(i):
                                c, a, b = pieces[i]
                                wA = take_slot()
                                piece_slot[i] = wA
                                wdt = b - a
                                cs = wA * 1024
                                t_sa = mm((cs, cs + wdt), A[:, cs:cs + wdt],
                                          k2[0:64, g, c * 128:c * 128 + 128],
                                          qfin[0:64, a:b],
                                          start=True, stop=True,
                                          waits=(t_qfin,) + k2_all)
                                t_sb = mm((cs + 512, cs + 512 + wdt),
                                          A[:, cs + 512:cs + 512 + wdt],
                                          k2[64:128, g, c * 128:c * 128 + 128],
                                          qfin[64:128, a:b],
                                          start=True, stop=True,
                                          waits=(t_qfin,) + k2_all)
                                t_e = act(lambda e, wA=wA, wdt=wdt, cs=cs:
                                          e.activation(
                                              pab[:, 2 * wA:2 * wA + 2, 0:wdt],
                                              A[:, cs:cs + 1024]
                                              .rearrange("p (s w) -> p s w", s=2)
                                              [:, :, 0:wdt],
                                              AF.Exp, bias=C_EXP_BIAS,
                                              scale=C_EXP_SCALE),
                                          waits=(t_sa, t_sb, pfree[wA]),
                                          psum_read=(cs, cs + 1024))
                                mtoks = []
                                mask_blocks = ((c, 0),) if is_g                                     else ((c, 0), (c + 2, 1))
                                for (blk, mi) in mask_blocks:
                                    bstart = blk * 128
                                    if bstart < a or bstart >= b:
                                        continue
                                    off = bstart - a
                                    for hslot in range(2):
                                        mtoks.append(dve(
                                            lambda e, hs=hslot, wA=wA,
                                                   off=off, mi=mi:
                                            e.tensor_tensor(
                                                pab[:, 2 * wA + hs,
                                                    off:off + 128],
                                                pab[:, 2 * wA + hs,
                                                    off:off + 128],
                                                masks[:, mi, :], OP.mult),
                                            waits=(t_e,)))
                                pend_av[wA] = (i, tuple(mtoks) if mtoks
                                               else (t_e,))

                            def emit_av(wA):
                                if pend_av[wA] is None:
                                    return
                                i, avw0 = pend_av[wA]
                                pend_av[wA] = None
                                c, a, b = pieces[i]
                                wdt = b - a
                                la = a - lo
                                lb = b - lo
                                avw = avw0 + v_all_tok + tuple(av_zero)
                                fresh = is_g and (c == 0)
                                is_last_c = (c == chunks[-1])
                                t1 = mm((3072 + la, 3072 + lb),
                                        A[0:64, 3072 + la:3072 + lb],
                                        v_all[:, c, g, :],
                                        pab[:, 2 * wA, 0:wdt],
                                        start=fresh, stop=is_last_c,
                                        skip=True, waits=avw,
                                        tile_position=(0, 0))
                                t2 = mm((3072 + la, 3072 + lb),
                                        A[64:128, 3072 + la:3072 + lb],
                                        v_all[:, c, g, :],
                                        pab[:, 2 * wA + 1, 0:wdt],
                                        start=fresh, stop=is_last_c,
                                        skip=True, waits=avw,
                                        tile_position=(0, 64))
                                t3 = mm((3584 + la, 3584 + lb),
                                        A[0:32, 3584 + la:3584 + lb],
                                        onesM[...], pab[:, 2 * wA, 0:wdt],
                                        start=fresh, stop=is_last_c,
                                        skip=True, waits=avw,
                                        tile_position=(0, 0))
                                t4 = mm((3584 + la, 3584 + lb),
                                        A[32:64, 3584 + la:3584 + lb],
                                        onesM[...],
                                        pab[:, 2 * wA + 1, 0:wdt],
                                        start=fresh, stop=is_last_c,
                                        skip=True, waits=avw,
                                        tile_position=(0, 32))
                                av_toks.extend((t1, t2))
                                rs_toks.extend((t3, t4))
                                pfree[wA] = t4

                            n_p = len(pieces)
                            ch_pos = {0: 0, 1: 1, 3: 2}
                            for i in range(n_p):
                                if i >= 2:
                                    emit_av(piece_slot[i - 2])
                                emit_scores(i)
                                if ch is not None and i in ch_pos:
                                    ch[ch_pos[i]]()
                            emit_av(piece_slot[n_p - 2])
                            emit_av(piece_slot[n_p - 1])
                            if ch is not None:
                                ch[3]()

                            t_l = act(lambda e: e.activation(
                                rcp[...], A[0:64, 3584:4096], AF.Ln),
                                waits=tuple(rs_toks[-2:]),
                                psum_read=(3584, 4096))
                            t_r = act(lambda e: e.activation(
                                rcp[...], rcp[...], AF.Exp, scale=-1.0),
                                waits=(t_l,))
                            t_nb = mm((2560, 3072), A[:, 2560:3072],
                                      sel2[...], rcp[...], start=True,
                                      stop=True, waits=(t_r,))
                            t_cp = dve(lambda e, qc=qc, lo=lo: e.tensor_copy(
                                OT[:, qc, lo:lo + 512], A[:, 3072:3584]),
                                waits=tuple(av_toks[-2:]),
                                psum_read=(3072, 3584))
                            t_mul = dve(lambda e, qc=qc, lo=lo:
                                        e.tensor_tensor(
                                            OT[:, qc, lo:lo + 512],
                                            OT[:, qc, lo:lo + 512],
                                            A[:, 2560:3072], OP.mult),
                                        waits=(t_nb, t_cp),
                                        psum_read=(2560, 3072))
                            ot_toks.append(t_mul)

                    res_src = scr0[s] if is_g else scrA
                    res_dst = scrA if is_g else scrB
                    src_gate = x0_done if is_g else res_done_g
                    wo_cast_tok = [wq_cast_tok[0], wq_cast_tok[1]]
                    t_res_st = []
                    prev_add = None
                    prev_sq = None
                    prev_ld = [None, None]
                    red_toks = []
                    for m in range(NCH):
                        par = m % 2
                        t_lw = kb.dma("sync", "LW%d" % par,
                                      wstage[:, 1024 * par:1024 * par + 1024]
                                      .rearrange("p (kc n) -> p kc n", kc=8),
                                      wo_d[m], waits=(wo_cast_tok[par],))
                        tw = []
                        for j in range(NCH):
                            tw.append(act(lambda e, j=j, par=par:
                                          e.activation(
                                              wob[:, par, j, :],
                                              wstage[:, 1024 * par + j * 128:
                                                      1024 * par + j * 128
                                                      + 128],
                                              AF.Copy),
                                          waits=(t_lw,)))
                        wo_cast_tok[par] = tw[-1]
                        tmm = []
                        for k in range(NCH):
                            for p in range(4):
                                tmm.append(mm(
                                    (p * 512, p * 512 + 512),
                                    A[:, p * 512:p * 512 + 512],
                                    wob[:, par, k, :],
                                    OT[:, k, p * 512:p * 512 + 512],
                                    start=(k == 0), stop=(k == NCH - 1),
                                    waits=(tw[k], ot_toks[-1])))
                        sp = par if is_g else 0
                        t_l1 = kb.dma("sync", "LS%d" % sp, strm[sp][...],
                                      res_src[m], waits=(prev_ld[sp],
                                                         src_gate))
                        wr = (t_l1, prev_sq) + tuple(tmm)
                        if t_res_st:
                            wr = wr + (t_res_st[-1],)
                        t_l2 = None
                        if not is_g:
                            t_l2 = kb.dma("sync", "LS1", strm[1][...],
                                          scr0[s][m],
                                          waits=(prev_ld[1], x0_done))
                        t_a = dve(lambda e, sp=sp: e.tensor_tensor(
                            qrope[...], A[:, 0:2048], strm[sp][...],
                            OP.add), waits=wr, psum_read=(0, 2048))
                        prev_ld[sp] = t_a
                        if not is_g:
                            t_a = dve(lambda e: e.tensor_tensor(
                                qrope[...], qrope[...], strm[1][...],
                                OP.add), waits=(t_a, t_l2))
                            prev_ld[1] = t_a
                        prev_add = t_a
                        sb = (m % 2) * 2048
                        t_sq = act(lambda e, sb=sb: e.activation(
                            sqb[:, sb:sb + 2048], qrope[...], AF.Square),
                            waits=(t_a,))
                        prev_sq = t_sq
                        rr = []
                        for p in range(4):
                            rr.append(mm(
                                (2048 + p * 512, 2048 + p * 512 + 512),
                                A[0:1, 2048 + p * 512:2048 + p * 512 + 512],
                                ones128[...],
                                sqb[:, sb + p * 512:sb + p * 512 + 512],
                                start=(m == 0), stop=(m == NCH - 1),
                                skip=True, waits=(t_sq,),
                                tile_position=(0, 0)))
                        if not is_g:
                            t_cb = act(lambda e, par=par: e.activation(
                                latb[:, par, :], qrope[...], AF.Copy),
                                waits=(t_a,))
                            prev_add = t_cb
                            prev_sq = t_cb
                            for p in range(4):
                                rr.append(mm(
                                    (2048 + p * 512,
                                     2048 + p * 512 + 512),
                                    A[32:33,
                                      2048 + p * 512:2048 + p * 512 + 512],
                                    ones128[...],
                                    latb[:, par, p * 512:p * 512 + 512],
                                    start=(m == 0), stop=(m == NCH - 1),
                                    skip=True, waits=(t_cb,),
                                    tile_position=(0, 32)))
                        red_toks = rr
                        t_st = kb.dma("scalar", "SRES", res_dst[m],
                                      qrope[...], waits=(prev_add,))
                        t_res_st.append(t_st)
                    res_done = t_res_st[-1]
                    if is_g:
                        res_done_g = res_done
                        red_toks_g = red_toks
                    else:
                        red_toks_l = red_toks
                        res_st_l = list(t_res_st)

                    if is_g:
                        t_bcast = []
                        for hh in range(2):
                            t_ln = act(lambda e, hh=hh: e.activation(
                                r1[...],
                                A[0:1, 2048 + hh * 1024:3072 + hh * 1024],
                                AF.Ln, bias=C_EPS_RMS, scale=C_INV_D),
                                waits=tuple(red_toks_g) + tuple(t_bcast),
                                psum_read=(2048 + hh * 1024,
                                           3072 + hh * 1024))
                            t_rms = act(lambda e: e.activation(
                                r2[...], r1[...], AF.Exp, scale=C_NHALF),
                                waits=(t_ln,))
                            t_h = dve(lambda e: e.tensor_copy(
                                rh[...], r2[...]),
                                waits=(t_rms,) + tuple(t_bcast))
                            t_lo = dve(lambda e: e.tensor_tensor(
                                rl[...], r2[...], rh[...], OP.subtract),
                                waits=(t_h,))
                            for p in range(2):
                                cc = hh * 1024 + p * 512
                                mm((cc, cc + 512), A[:, cc:cc + 512],
                                   onesk1b[...],
                                   rh[:, p * 512:p * 512 + 512],
                                   start=True, stop=False, waits=(t_lo,))
                                t_bcast.append(mm((cc, cc + 512),
                                                  A[:, cc:cc + 512],
                                                  onesk1b[...],
                                                  rl[:, p * 512:p * 512 + 512],
                                                  start=False, stop=True,
                                                  waits=(t_lo,)))
                        t_xn = [None] * NCH
                        prevl = [prev_ld[0], prev_ld[1]]
                        for k in range(NCH):
                            par = k % 2
                            t_l = kb.dma("sync", "LS%d" % par,
                                         strm[par][...], scrA[k],
                                         waits=(prevl[par], t_res_st[k]))
                            t_xn[k] = dve(lambda e, kk=k, p=par:
                                          e.tensor_tensor(
                                              xnT[:, kk, :], strm[p][...],
                                              A[:, 0:2048], OP.mult),
                                          waits=(t_l,) + tuple(t_bcast),
                                          psum_read=(0, 2048))
                            prevl[par] = t_xn[k]
                        t_xn_all = tuple(t_xn)

                kb.flush()


            fst = ExitStack()
            with fst:
                Tf = lambda n, *a: fst.enter_context(nc.sbuf_tensor(n + "f%d" % s, *a))
                w2b = Tf("w2b", [128, 32, 1024], BF16)
                hT = Tf("hT", [128, 32, 512], BF16)
                w1mb = Tf("w1mb", [128, 2, NCH, 128], BF16)
                zpan = Tf("zpan", [128, NCH, 512], F32)
                zbf = hT[:, 24:32, :]
                f2048 = zpan[:, 0:4, :].rearrange("p a b -> p (a b)")
                strmF = [Tf("strmF0", [128, S], F32),
                         Tf("strmF1", [128, S], F32)]

                kb.barrier()

                t_stats = red_toks_l
                t_ab = []
                t_bb = []
                for hh in range(2):
                    h0 = hh * 1024
                    t_mean = act(lambda e, h0=h0: e.activation(
                        r1[...], A[32:33, 2048 + h0:2048 + h0 + 1024],
                        AF.Copy, scale=C_INV_D),
                        waits=tuple(t_stats) + tuple(t_ab) + tuple(t_bb),
                        psum_read=(2048 + h0, 2048 + h0 + 1024))
                    t_msq = act(lambda e: e.activation(
                        r2[...], r1[...], AF.Square), waits=(t_mean,))
                    t_v1 = dve(lambda e, h0=h0: e.tensor_scalar(
                        r3[...], A[0:1, 2048 + h0:2048 + h0 + 1024],
                        C_INV_D, None, OP.mult),
                        waits=tuple(t_stats) + tuple(t_bb),
                        psum_read=(2048 + h0, 2048 + h0 + 1024))
                    t_v2 = dve(lambda e: e.tensor_tensor(
                        r3[...], r3[...], r2[...], OP.subtract),
                        waits=(t_v1, t_msq))
                    t_lnr = act(lambda e: e.activation(
                        r2[...], r3[...], AF.Ln, bias=C_EPS_LN),
                        waits=(t_v2,))
                    t_a = act(lambda e: e.activation(
                        r2[...], r2[...], AF.Exp, scale=C_NHALF),
                        waits=(t_lnr,))
                    t_b1_ = dve(lambda e: e.tensor_tensor(
                        r3[...], r1[...], r2[...], OP.mult), waits=(t_a,))
                    t_b = dve(lambda e: e.tensor_scalar(
                        r3[...], r3[...], -1.0, None, OP.mult),
                        waits=(t_b1_,))
                    t_h = dve(lambda e: e.tensor_copy(rh[...], r2[...]),
                              waits=(t_a,) + tuple(t_ab))
                    t_lo = dve(lambda e: e.tensor_tensor(
                        rl[...], r2[...], rh[...], OP.subtract), waits=(t_h,))
                    for p in range(2):
                        cc = h0 + p * 512
                        mm((cc, cc + 512), A[:, cc:cc + 512], onesk1b[...],
                           rh[:, p * 512:p * 512 + 512],
                           start=True, stop=False, waits=(t_lo,))
                        t_ab.append(mm((cc, cc + 512), A[:, cc:cc + 512],
                                       onesk1b[...],
                                       rl[:, p * 512:p * 512 + 512],
                                       start=False, stop=True, waits=(t_lo,)))
                    t_h2 = dve(lambda e: e.tensor_copy(rh[...], r3[...]),
                               waits=(t_b,) + tuple(t_ab) + tuple(t_bb))
                    t_lo2 = dve(lambda e: e.tensor_tensor(
                        rl[...], r3[...], rh[...], OP.subtract), waits=(t_h2,))
                    for p in range(2):
                        cc = h0 + p * 512
                        mm((2048 + cc, 2048 + cc + 512),
                           A[:, 2048 + cc:2048 + cc + 512], onesk1b[...],
                           rh[:, p * 512:p * 512 + 512],
                           start=True, stop=False, waits=(t_lo2,))
                        t_bb.append(mm((2048 + cc, 2048 + cc + 512),
                                       A[:, 2048 + cc:2048 + cc + 512],
                                       onesk1b[...],
                                       rl[:, p * 512:p * 512 + 512],
                                       start=False, stop=True,
                                       waits=(t_lo2,)))
                t_xnf = []
                prev_t1p = [None, None]
                for k in range(NCH):
                    par = k % 2
                    t_l = kb.dma("sync", "LSF", strmF[par][...], scrB[k],
                                 waits=(prev_t1p[par], strmF_guard[par],
                                        res_st_l[k]))
                    t1 = dve(lambda e, p=par: e.tensor_tensor(
                        f2048, strmF[p][...], A[:, 0:2048], OP.mult),
                        waits=(t_l,) + tuple(t_ab) +
                        ((t_xnf[-1],) if t_xnf else ()),
                        psum_read=(0, 2048))
                    t2 = dve(lambda e: e.tensor_tensor(
                        f2048, f2048, A[:, 2048:4096], OP.add),
                        waits=tuple(t_bb), psum_read=(2048, 4096))
                    t3 = act(lambda e, kk=k: e.activation(
                        xnT[:, kk, :], f2048, AF.Copy), waits=(t2,))
                    t_xnf.append(t3)
                    prev_t1p[par] = t1
                ap_last = [prev_t1p[0], prev_t1p[1]]

                prev_cp2 = [None, None]
                prev_cpf = [strmF_guard[0], strmF_guard[1]]
                w2_toks = []
                for piece in range(32):
                    pr = piece % 2
                    if piece < 12:
                        stg = wstage[:, pr * 1024:pr * 1024 + 1024]
                        t_l = kb.dma("gpsimd", "W2L%d" % pr, stg,
                                     wd["w2c"][:, piece, :],
                                     waits=(prev_cp2[pr],))
                        tcp = pool(lambda e, pc=piece, stg=stg:
                                   e.tensor_copy(w2b[:, pc, :], stg),
                                   waits=(t_l,))
                        prev_cp2[pr] = tcp
                    else:
                        stg = strmF[pr][:, 0:1024]
                        t_l = kb.dma("gpsimd", "W2L%d" % pr, stg,
                                     wd["w2c"][:, piece, :],
                                     waits=(prev_cpf[pr], ap_last[pr]))
                        tcp = pool(lambda e, pc=piece, stg=stg:
                                   e.tensor_copy(w2b[:, pc, :], stg),
                                   waits=(t_l,))
                        prev_cpf[pr] = tcp
                    w2_toks.append(tcp)
                strmF_guard = [prev_cpf[0], prev_cpf[1]]

                w1_cast_tok = [w2_toks[10], w2_toks[11]]
                tail_hooks = {}
                tl_prev = None
                prev_cp = None
                prev_st = None

                for pan in range(4):
                    plo = pan * 512
                    gel_toks = []
                    for mdf in range(32):
                        for th in tail_hooks.pop(mdf, ()):
                            th()
                        par = mdf % 2
                        t_lw = kb.dma("sync", "LW%d" % par,
                                      wstage[:, 1024 * par:1024 * par + 1024]
                                      .rearrange("p (kc n) -> p kc n", kc=8),
                                      wd["w1c"][mdf], waits=(w1_cast_tok[par],))
                        t_c = dve(lambda e, par=par: e.tensor_copy(
                            w1mb[:, par, :, :],
                            wstage[:, 1024 * par:1024 * par + 1024]
                            .rearrange("p (kc n) -> p kc n", kc=8)),
                            waits=(t_lw,))
                        c0 = (mdf % 4) * 512
                        tmm = []
                        for k in range(NCH):
                            tmm.append(mm((c0, c0 + 512),
                                          A[:, c0:c0 + 512],
                                          w1mb[:, par, k, :],
                                          xnT[:, k, plo:plo + 512],
                                          start=(k == 0), stop=(k == NCH - 1),
                                          waits=(t_c, t_xnf[k])))
                        w1_cast_tok[par] = tmm[-1]
                        ggate = ((tl_prev["tsm"],)
                                 if (tl_prev is not None and mdf >= 24)
                                 else ())
                        gel_toks.append(act(
                            lambda e, mdf=mdf, c0=c0: e.activation(
                                hT[:, mdf, :], A[:, c0:c0 + 512],
                                AF.Gelu_apprx_tanh, bias=b1t[:, mdf:mdf + 1]),
                            waits=tuple(tmm) + ggate,
                            psum_read=(c0, c0 + 512)))
                    for kk in sorted(tail_hooks):
                        for th in tail_hooks.pop(kk):
                            th()
                    ztoks = []
                    for m in range(NCH):
                        tmmY = []
                        for k in range(32):
                            tmmY.append(mm((m * 512, m * 512 + 512),
                                           A[:, m * 512:m * 512 + 512],
                                           w2b[:, k, m * 128:m * 128 + 128],
                                           hT[:, k, :],
                                           start=(k == 0), stop=(k == 31),
                                           waits=(w2_toks[k],
                                                  gel_toks[k])))
                        t_e = dve(lambda e, m=m: e.tensor_scalar(
                            zpan[:, m, :], A[:, m * 512:m * 512 + 512],
                            b2t[:, m:m + 1], None, OP.add),
                            waits=(tmmY[-1],),
                            psum_read=(m * 512, m * 512 + 512))
                        t_z = dve(lambda e, m=m, plo=plo: e.tensor_tensor(
                            zpan[:, m, :], zpan[:, m, :],
                            xnT[:, m, plo:plo + 512],
                            OP.add), waits=(t_e, t_xnf[m]))
                        ztoks.append(t_z)
                    zb_h = [dve(lambda e, h=h: e.tensor_copy(
                        zbf[:, 4 * h:4 * h + 4, :],
                        zpan[:, 4 * h:4 * h + 4, :]),
                        waits=tuple(ztoks[4 * h:4 * h + 4]))
                        for h in range(2)]
                    zq_h = [act(lambda e, h=h: e.activation(
                        sqb[:, 2048 * h:2048 * h + 2048],
                        zpan[:, 4 * h:4 * h + 4, :]
                        .rearrange("p a b -> p (a b)"),
                        AF.Square), waits=tuple(ztoks[4 * h:4 * h + 4]))
                        for h in range(2)]
                    t_zb = zb_h
                    t_zq = zq_h

                    tl = {}

                    def th_reduce(t_zb=t_zb, t_zq=t_zq, tl=tl):
                        tsm = []
                        tsq = []
                        for m in range(NCH):
                            tsm.append(mm((2048, 2560), A[0:1, 2048:2560],
                                          ones128[...],
                                          zbf[:, m, :], start=(m == 0),
                                          stop=(m == NCH - 1),
                                          waits=(t_zb[m // 4],)))
                            tsq.append(mm((2560, 3072), A[0:1, 2560:3072],
                                          ones128[...],
                                          sqb[:, m * 512:m * 512 + 512],
                                          start=(m == 0),
                                          stop=(m == NCH - 1),
                                          waits=(t_zq[m // 4],)))
                        tl["tsm"] = tsm[-1]
                        tl["tsq"] = tsq[-1]

                    def th_chain(tl=tl):
                        t_mean = act(lambda e: e.activation(
                            r1[:, 0:512], A[0:1, 2048:2560], AF.Copy,
                            scale=C_INV_D), waits=(tl["tsm"],),
                            psum_read=(2048, 2560))
                        t_msq = act(lambda e: e.activation(
                            r2[:, 0:512], r1[:, 0:512], AF.Square),
                            waits=(t_mean,))
                        t_v1 = dve(lambda e: e.tensor_scalar(
                            r3[:, 0:512], A[0:1, 2560:3072], C_INV_D, None,
                            OP.mult), waits=(tl["tsq"],),
                            psum_read=(2560, 3072))
                        t_v2 = dve(lambda e: e.tensor_tensor(
                            r3[:, 0:512], r3[:, 0:512], r2[:, 0:512],
                            OP.subtract), waits=(t_v1, t_msq))
                        t_lnr = act(lambda e: e.activation(
                            r2[:, 0:512], r3[:, 0:512], AF.Ln,
                            bias=C_EPS_LN), waits=(t_v2,))
                        t_a = act(lambda e: e.activation(
                            r2[:, 0:512], r2[:, 0:512], AF.Exp,
                            scale=C_NHALF), waits=(t_lnr,))
                        t_b1_ = dve(lambda e: e.tensor_tensor(
                            r3[:, 0:512], r1[:, 0:512], r2[:, 0:512],
                            OP.mult), waits=(t_a,))
                        t_b = dve(lambda e: e.tensor_scalar(
                            r3[:, 0:512], r3[:, 0:512], -1.0,
                            None, OP.mult), waits=(t_b1_,))
                        t_h = dve(lambda e: e.tensor_copy(
                            rh[:, 0:512], r2[:, 0:512]), waits=(t_a,))
                        t_lo = dve(lambda e: e.tensor_tensor(
                            rl[:, 0:512], r2[:, 0:512], rh[:, 0:512],
                            OP.subtract), waits=(t_h,))
                        tl["t_lo"] = t_lo
                        tl["t_b"] = t_b

                    def th_bcast(tl=tl):
                        mm((2048, 2560), A[:, 2048:2560], onesk1b[...],
                           rh[:, 0:512], start=True, stop=False,
                           waits=(tl["t_lo"],))
                        t_abm = mm((2048, 2560), A[:, 2048:2560],
                                   onesk1b[...], rl[:, 0:512],
                                   start=False, stop=True,
                                   waits=(tl["t_lo"],))
                        t_h2 = dve(lambda e: e.tensor_copy(
                            rh[:, 512:1024], r3[:, 0:512]),
                            waits=(tl["t_b"], t_abm))
                        t_lo2 = dve(lambda e: e.tensor_tensor(
                            rl[:, 512:1024], r3[:, 0:512], rh[:, 512:1024],
                            OP.subtract), waits=(t_h2,))
                        mm((2560, 3072), A[:, 2560:3072], onesk1b[...],
                           rh[:, 512:1024], start=True, stop=False,
                           waits=(t_lo2,))
                        t_bbm = mm((2560, 3072), A[:, 2560:3072],
                                   onesk1b[...], rl[:, 512:1024],
                                   start=False, stop=True, waits=(t_lo2,))
                        tl["t_abm"] = t_abm
                        tl["t_bbm"] = t_bbm

                    def th_apply(t_zb=t_zb, t_zq=t_zq, tl=tl):
                        ap_toks = []
                        for m in range(NCH):
                            t1 = dve(lambda e, m=m: e.tensor_tensor(
                                zpan[:, m, :], zpan[:, m, :],
                                A[:, 2048:2560],
                                OP.mult),
                                waits=(tl["t_abm"], t_zb[1], t_zq[1]),
                                psum_read=(2048, 2560))
                            t2 = dve(lambda e, m=m: e.tensor_tensor(
                                zpan[:, m, :], zpan[:, m, :],
                                A[:, 2560:3072],
                                OP.add), waits=(tl["t_bbm"],),
                                psum_read=(2560, 3072))
                            ap_toks.append(t2)
                        tl["ap"] = tuple(ap_toks)

                    def mk_trans(tb, s=s, plo=plo, tl=tl):
                        def th_trans():
                            nonlocal prev_cp, prev_st
                            preg = 3072
                            ttr = []
                            for k in range(NCH):
                                cc = preg + k * 128
                                ttr.append(trans((cc, cc + 128),
                                                 A[:, cc:cc + 128],
                                                 zpan[:, k,
                                                      tb * 128:tb * 128 + 128],
                                                 waits=tl["ap"]))
                            t_cp = dve(lambda e, preg=preg: e.tensor_copy(
                                outstage[...], A[:, preg:preg + 1024]),
                                waits=tuple(ttr) + (prev_st,),
                                psum_read=(preg, preg + 1024))
                            t_stt = kb.dma("scalar", "SOU",
                                           y_out[s, plo + tb * 128:
                                                 plo + tb * 128 + 128, :],
                                           outstage[...], waits=(t_cp,))
                            prev_st = t_stt
                            prev_cp = t_cp
                        return th_trans

                    hooks = {1: [th_reduce], 3: [th_chain], 9: [th_bcast],
                             11: [th_apply], 15: [mk_trans(0)],
                             17: [mk_trans(1)], 19: [mk_trans(2)],
                             21: [mk_trans(3)]}
                    tl_prev = tl
                    if pan < 3:
                        tail_hooks = hooks
                    else:
                        for kk in sorted(hooks):
                            for th in hooks[kk]:
                                th()
                        tl_prev = None

                kb.flush()

        finals = [(nm, kb.cnt[nm]) for nm in ("SOU", "SRES", "KD")
                  if kb.cnt[nm] > 0]
        for eng in ("sync", "scalar", "gpsimd"):
            kb.wait(eng, *finals)
        kb.flush()
    return nc, hc


_CACHED = None


def make_in_maps(hc, L0_attn_hidden, L0_attn_norm1_w, L0_attn_norm2_w,
                 L0_attn_g_wq, L0_attn_g_kv_down, L0_attn_g_k_up,
                 L0_attn_g_v_up, L0_attn_g_wo, L0_attn_l_wq, L0_attn_l_wk,
                 L0_attn_l_wv, L0_attn_l_wo, L0_ffn_w1, L0_ffn_b1, L0_ffn_w2,
                 L0_ffn_b2, cores=8):
    x = np.ascontiguousarray(np.asarray(L0_attn_hidden, dtype=np.float32))
    cmaps = {}
    for nm in ("ropeg", "ropel", "perm", "iden", "masks", "hsel", "sel16",
               "sel2", "onesM", "ones128", "onesk1b", "zrow", "cpool"):
        cmaps[nm] = np.ascontiguousarray(hc[nm])
    n1 = np.asarray(L0_attn_norm1_w, np.float32)
    n2 = np.asarray(L0_attn_norm2_w, np.float32)
    cmaps["n1t"] = np.ascontiguousarray(n1.reshape(8, 128).T)
    cmaps["n2t"] = np.ascontiguousarray(n2.reshape(8, 128).T)
    def tile_sq(w):
        return np.ascontiguousarray(
            np.asarray(w, np.float32).reshape(8, 128, 8, 128)
            .transpose(2, 1, 0, 3))

    def tile_kv(w, kc):
        a = np.asarray(w, np.float32)
        return np.ascontiguousarray(
            a.reshape(kc, 128, a.shape[1]).transpose(1, 0, 2))

    wmap = dict(gwq=tile_sq(L0_attn_g_wq), gkvd=tile_kv(L0_attn_g_kv_down, 8),
                gkup=tile_kv(L0_attn_g_k_up, 2), gvup=tile_kv(L0_attn_g_v_up, 2),
                gwo=tile_sq(L0_attn_g_wo), lwq=tile_sq(L0_attn_l_wq),
                lwk=tile_kv(L0_attn_l_wk, 8), lwv=tile_kv(L0_attn_l_wv, 8),
                lwo=tile_sq(L0_attn_l_wo))
    in_maps = []
    for c in range(cores):
        m = dict(cmaps)
        m["x_in"] = np.ascontiguousarray(x[2 * c:2 * c + 2])
        for k, v in wmap.items():
            m[k] = v
        m["w1c"] = np.ascontiguousarray(
            np.asarray(L0_ffn_w1[c], np.float32).reshape(8, 128, 32, 128)
            .transpose(2, 1, 0, 3))
        m["w2c"] = np.ascontiguousarray(
            np.asarray(L0_ffn_w2[c], np.float32).reshape(32, 128, 1024)
            .transpose(1, 0, 2))
        m["b1t"] = np.ascontiguousarray(
            np.asarray(L0_ffn_b1[c], np.float32).reshape(32, 128).T)
        m["b2t"] = np.ascontiguousarray(
            np.asarray(L0_ffn_b2[c], np.float32).reshape(8, 128).T)
        in_maps.append(m)
    return in_maps


def kernel(**inputs):
    global _CACHED
    if _CACHED is None:
        _CACHED = build_core_program()
    nc, hc = _CACHED
    in_maps = make_in_maps(hc, **inputs)
    res = run_bass_kernel_spmd(nc, in_maps, core_ids=list(range(8)))
    out = np.concatenate([r["y_out"] for r in res.results], axis=0)
    return np.ascontiguousarray(out.astype(np.float32))


if __name__ == "__main__":
    nc, _ = build_core_program()
    print("build ok; instructions:", nc.n_instructions())


# version 42 (final)
# speedup vs baseline: 1.0026x; 1.0026x over previous
import numpy as np
import ml_dtypes
from contextlib import ExitStack

import concourse.bass as bass
import concourse.mybir as mybir
from concourse.bass_utils import run_bass_kernel_spmd

F32 = mybir.dt.float32
BF16 = mybir.dt.bfloat16
AF = mybir.ActivationFunctionType
OP = mybir.AluOpType
BF = ml_dtypes.bfloat16

S = 2048
D = 1024
H = 16
KVH = 4
HD = 64
LAT = 256
WIN = 256
DFF = 4096
NSEQ = 2
NCH = 8
NTC = 16
C_EXP_SCALE = 0.125
C_EXP_BIAS = -8.0
C_INV_D = 1.0 / 1024.0
C_INV_HD = 0.015625
C_EPS_RMS = 1e-6
C_EPS_LN = 1e-5
C_NHALF = -0.5

ENGINES = ("sync", "tensor", "vector", "scalar", "gpsimd")


def _jax_rope_tables(base):
    import jax
    import jax.numpy as jnp
    with jax.default_device(jax.devices("cpu")[0]):
        inv_freq = 1.0 / base ** (jnp.arange(0, HD, 2, dtype=jnp.float32) / HD)
        freqs = jnp.arange(S, dtype=jnp.float32)[:, None] * inv_freq[None, :]
        cos, sin = jnp.cos(freqs), jnp.sin(freqs)
        return (np.asarray(cos, np.float32).T.copy(),
                np.asarray(sin, np.float32).T.copy())


def host_constants():
    c = {}
    for base, name in ((1e6, "ropeg"), (1e4, "ropel")):
        cos, sin = _jax_rope_tables(base)
        Ct = np.zeros((128, 2, S), np.float32)
        for p in range(128):
            hd = p % 64
            j = hd // 2
            Ct[p, 0] = cos[j]
            Ct[p, 1] = -sin[j] if hd % 2 == 0 else sin[j]
        c[name] = Ct
    perm = np.zeros((128, 128), BF)
    for m in range(128):
        perm[m ^ 1, m] = 1.0
    c["perm"] = perm
    c["iden"] = np.eye(128, dtype=np.float32)
    i = np.arange(128)
    mdiag = (i[None, :] >= i[:, None]).astype(BF)
    mlow = (i[None, :] < i[:, None]).astype(BF)
    c["masks"] = np.ascontiguousarray(np.stack([mdiag, mlow], axis=1))
    hsel = np.zeros((128, 8, 16), BF)
    for ch in range(8):
        hsel[0:64, ch, 2 * ch] = 1.0
        hsel[64:128, ch, 2 * ch + 1] = 1.0
    c["hsel"] = hsel
    sel16 = np.zeros((16, 8, 128), BF)
    for ch in range(8):
        sel16[2 * ch, ch, 0:64] = 1.0
        sel16[2 * ch + 1, ch, 64:128] = 1.0
    c["sel16"] = sel16
    sel2 = np.zeros((64, 128), BF)
    sel2[0, 0:64] = 1.0
    sel2[32, 64:128] = 1.0
    c["sel2"] = sel2
    c["onesM"] = np.ones((128, 32), BF)
    c["ones128"] = np.ones((128, 1), BF)
    c["onesk1b"] = np.ones((1, 128), BF)
    c["zrow"] = np.zeros((1, 576), BF)
    cp = np.zeros((128, 16), np.float32)
    vals = [C_EXP_SCALE, C_EXP_BIAS, C_INV_D, C_EPS_RMS, C_NHALF, C_INV_HD,
            C_EPS_LN, 1.0, 0.0, 0.5, -1.0]
    for j, v in enumerate(vals):
        cp[:, j] = v
    c["cpool"] = cp
    c["_cvals"] = vals
    return c


class KB:
    def __init__(self, nc, stack):
        self.nc = nc
        self.pend = []
        self.cnt = {}
        self.hwm = {}
        self.sems = {}
        self.bgate = [dict() for _ in range(8)]
        for name in ("LDC", "LS0", "LS1", "LW0", "LW1", "LX0", "LX1",
                     "LSF", "W2L0", "W2L1", "SOU", "SRES", "SX3", "KD",
                     "PE", "ACT", "DVE", "POOL", "SY"):
            self.sems[name] = stack.enter_context(nc.semaphore(name))
            self.cnt[name] = 0

    def op(self, eng, fn, waits=(), inc=None, amt=1):
        merged = {}
        for t in waits:
            if t is None:
                continue
            s, v = t
            if merged.get(s, 0) < v:
                merged[s] = v
        w2 = []
        for s, v in merged.items():
            key = (eng, s)
            if self.hwm.get(key, 0) < v:
                self.hwm[key] = v
                w2.append((s, v))
        tok = None
        if inc:
            self.cnt[inc] += amt
            tok = (inc, self.cnt[inc])
        self.pend.append((eng, fn, w2, (inc, amt) if inc else None))
        return tok

    def wait(self, eng, *toks):
        for t in toks:
            if t is None:
                continue
            self.op(eng, None, waits=(t,))

    def dma(self, eng, sem, out, in_, waits=()):
        return self.op(eng, lambda e, o=out, i=in_: e.dma_start(out=o, in_=i),
                       waits, inc=sem, amt=16)

    def bank_of(self, col):
        return col // 512

    def gates(self, c0, c1):
        out = {}
        for b in range(self.bank_of(c0), self.bank_of(c1 - 1) + 1):
            for s, v in self.bgate[b].items():
                if out.get(s, 0) < v:
                    out[s] = v
        return tuple(out.items())

    def read_psum(self, c0, c1, tok):
        if tok is None:
            return
        s, v = tok
        for b in range(self.bank_of(c0), self.bank_of(c1 - 1) + 1):
            if self.bgate[b].get(s, 0) < v:
                self.bgate[b][s] = v

    def barrier(self):
        toks = []
        ctr = {"sync": "SY", "tensor": "PE", "vector": "DVE",
               "scalar": "ACT", "gpsimd": "POOL"}
        for eng in ENGINES:
            toks.append(self.op(eng, lambda e: e.nop(), inc=ctr[eng]))
        for eng in ENGINES:
            self.wait(eng, *toks)
        for b in range(8):
            for t in toks:
                s, v = t
                if self.bgate[b].get(s, 0) < v:
                    self.bgate[b][s] = v

    def flush(self):
        pend = self.pend
        self.pend = []
        sems = self.sems
        with self.nc.Block() as blk:
            for eng in ENGINES:
                ops = [p for p in pend if p[0] == eng]
                if not ops:
                    continue

                def body(e, ops=ops):
                    for (_, fn, ws, inc) in ops:
                        for (s, v) in ws:
                            e.wait_ge(sems[s], v)
                        if fn is None:
                            continue
                        inst = fn(e)
                        if inc is not None:
                            inst.then_inc(sems[inc[0]], inc[1])
                getattr(blk, eng)(body)


def build_core_program(debug=False):
    nc = bass.Bass("TRN2", target_bir_lowering=False,
                   detect_race_conditions=False)
    hc = host_constants()

    x_in = nc.dram_tensor("x_in", [NSEQ, S, D], F32, kind="ExternalInput").ap()
    y_out = nc.dram_tensor("y_out", [NSEQ, S, D], F32, kind="ExternalOutput").ap()
    wd = {}
    for nm, shp in (("gwq", [8, 128, 8, 128]), ("gkvd", [128, 8, LAT]),
                    ("gkup", [128, 2, LAT]), ("gvup", [128, 2, LAT]),
                    ("gwo", [8, 128, 8, 128]), ("lwq", [8, 128, 8, 128]),
                    ("lwk", [128, 8, LAT]), ("lwv", [128, 8, LAT]),
                    ("lwo", [8, 128, 8, 128]), ("w1c", [32, 128, 8, 128]),
                    ("w2c", [128, 32, 1024])):
        wd[nm] = nc.dram_tensor(nm, shp, F32, kind="ExternalInput").ap()
    n1t_d = nc.dram_tensor("n1t", [128, 8], F32, kind="ExternalInput").ap()
    n2t_d = nc.dram_tensor("n2t", [128, 8], F32, kind="ExternalInput").ap()
    b1t_d = nc.dram_tensor("b1t", [128, 32], F32, kind="ExternalInput").ap()
    b2t_d = nc.dram_tensor("b2t", [128, 8], F32, kind="ExternalInput").ap()
    cd = {}
    for nm in ("ropeg", "ropel", "perm", "iden", "masks", "hsel", "sel16",
               "sel2", "onesM", "ones128", "onesk1b", "zrow", "cpool"):
        arr = hc[nm]
        dt = BF16 if arr.dtype == BF else F32
        cd[nm] = nc.dram_tensor(nm, list(arr.shape), dt, kind="ExternalInput").ap()

    if debug:
        dbg_t = nc.dram_tensor("dbg_t", [NCH, 128, S], F32, kind="ExternalOutput").ap()
        dbg_x3 = nc.dram_tensor("dbg_x3", [NCH, 128, S], F32, kind="ExternalOutput").ap()
        dbg_xn = nc.dram_tensor("dbg_xn", [NCH, 128, S], F32, kind="ExternalOutput").ap()
        dbg_rows = nc.dram_tensor("dbg_rows", [2, S], F32, kind="ExternalOutput").ap()
        dbg_q = nc.dram_tensor("dbg_q", [128, S], BF16, kind="ExternalOutput").ap()
        dbg_k = nc.dram_tensor("dbg_k", [128, KVH, S], BF16, kind="ExternalOutput").ap()
    scr0 = nc.dram_tensor("scr0", [NSEQ, NCH, 128, S], F32).ap()
    scrA = nc.dram_tensor("scrA", [NCH, 128, S], F32).ap()
    scrB = nc.dram_tensor("scrB", [NCH, 128, S], F32).ap()

    st = ExitStack()
    with st:
        kb = KB(nc, st)
        T = lambda *a: st.enter_context(nc.sbuf_tensor(*a))
        xnT = T("xnT", [128, NCH, S], BF16)
        r1 = T("r1", [1, 1024], F32)
        r2 = T("r2", [1, 1024], F32)
        r3 = T("r3", [1, 1024], F32)
        rh = T("rh", [1, 1024], BF16)
        rl = T("rl", [1, 1024], BF16)
        wstage = T("wstage", [128, S], F32)
        outstage = T("outstage", [128, 1024], F32)
        sqb = T("sqb", [128, 2 * S], BF16)
        cpool = T("cpool_s", [128, 16], F32)
        n1t = T("n1t_s", [128, 8], F32)
        n2t = T("n2t_s", [128, 8], F32)
        b1t = T("b1t_s", [128, 32], F32)
        b2t = T("b2t_s", [128, 8], F32)
        perm = T("perm_s", [128, 128], BF16)
        iden = T("iden_s", [128, 128], F32)
        masks = T("masks_s", [128, 2, 128], BF16)
        hsel = T("hsel_s", [128, 8, 16], BF16)
        sel16 = T("sel16_s", [16, 8, 128], BF16)
        sel2 = T("sel2_s", [64, 128], BF16)
        onesM = T("onesM_s", [128, 32], BF16)
        ones128 = T("ones128_s", [128, 1], BF16)
        onesk1b = T("onesk1b_s", [1, 128], BF16)
        zrow = T("zrow_s", [1, 576], BF16)
        A = st.enter_context(nc.psum_tensor("arena", [128, 4096], F32))

        for j, v in enumerate(hc["_cvals"]):
            nc.const_aps.register(F32, float(v), cpool[:, j:j + 1])

        def mm(out_cols, out_ap, lhsT, rhs, start, stop, waits=(), skip=False,
               tile_position=None):
            c0, c1 = out_cols
            w = tuple(waits) + kb.gates(c0, c1)
            return kb.op("tensor",
                         lambda e, o=out_ap, l=lhsT, r=rhs, st_=start, sp=stop,
                                tp=tile_position, sk=skip:
                         e.matmul(o, l, r, start=st_, stop=sp,
                                  tile_position=tp, skip_group_check=sk),
                         w, inc="PE")

        def trans(out_cols, out_ap, in_ap, waits=()):
            c0, c1 = out_cols
            w = tuple(waits) + kb.gates(c0, c1)
            return kb.op("tensor",
                         lambda e, o=out_ap, i=in_ap: e.transpose(o, i, iden[...]),
                         w, inc="PE")

        def dve(fn, waits=(), psum_read=None):
            t = kb.op("vector", fn, waits, inc="DVE")
            if psum_read:
                kb.read_psum(psum_read[0], psum_read[1], t)
            return t

        def act(fn, waits=(), psum_read=None):
            t = kb.op("scalar", fn, waits, inc="ACT")
            if psum_read:
                kb.read_psum(psum_read[0], psum_read[1], t)
            return t

        def pool(fn, waits=()):
            return kb.op("gpsimd", fn, waits, inc="POOL")

        ld = []
        for nm, dst in (("perm", perm), ("iden", iden), ("masks", masks),
                        ("hsel", hsel), ("sel16", sel16), ("sel2", sel2),
                        ("ones128", ones128), ("onesM", onesM),
                        ("onesk1b", onesk1b),
                        ("zrow", zrow), ("cpool", cpool)):
            ld.append(kb.dma("sync", "LDC", dst[...], cd[nm]))
        ld.append(kb.dma("sync", "LDC", n1t[...], n1t_d))
        ld.append(kb.dma("sync", "LDC", n2t[...], n2t_d))
        ld.append(kb.dma("sync", "LDC", b1t[...], b1t_d))
        ld.append(kb.dma("sync", "LDC", b2t[...], b2t_d))
        for eng in ("vector", "scalar", "tensor", "gpsimd"):
            kb.wait(eng, ld[-1])

        zl64 = zrow[0:1, 0:64]
        zl1 = zrow[0:1, 0:1]
        zl32 = zrow[0:1, 0:32]
        zr512 = zrow[0:1, 64:576]

        def cast_weight(dram, dst, nkc, ncols, fold=None, waits=(),
                        eng=None):
            eng = eng or dve
            t_l = kb.dma("sync", "LW0", wstage[:, 0:nkc * ncols]
                         .rearrange("p (kc n) -> p kc n", kc=nkc), dram,
                         waits=waits)
            toks = []
            for j in range(nkc):
                stg = wstage[:, j * ncols:(j + 1) * ncols]
                dst_ap = dst[:, j, 0:ncols]
                if fold is not None:
                    toks.append(eng(lambda e, d=dst_ap, s=stg,
                                    f=fold[:, j:j + 1]:
                                    e.tensor_scalar(d, s, f, None, OP.mult),
                                    waits=(t_l,)))
                else:
                    toks.append(eng(lambda e, d=dst_ap, s=stg:
                                    e.tensor_copy(d, s), waits=(t_l,)))
            return toks

        strmF_guard = [None, None]
        for s in range(NSEQ):
            ast = ExitStack()
            with ast:
                Ta = lambda n, *a: ast.enter_context(nc.sbuf_tensor(n + "_%d" % s, *a))
                k2 = Ta("k2", [128, KVH, S], BF16)
                v_all = Ta("v_all", [128, NTC, KVH, HD], BF16)
                qbf = Ta("qbf", [128, S], BF16)
                qrope = Ta("qrope", [128, S], F32)
                qfin = Ta("qfin", [128, S], BF16)
                pab = Ta("pab", [128, 6, 512], BF16)
                t1buf = pab[:, 0:4, :].rearrange("p a b -> p (a b)")
                OT = Ta("OT", [128, NCH, S], BF16)
                wqb = Ta("wqb", [128, 2, NCH, 128], BF16)
                wob = Ta("wob", [128, 2, NCH, 128], BF16)
                wkvb = Ta("wkvb", [128, NCH, LAT], BF16)
                wvb = Ta("wvb", [128, NCH, LAT], BF16)
                ropeC = Ta("ropeC", [128, S], BF16)
                ropeS = sqb[:, 2048:4096]
                rstd16f = Ta("rstd16f", [16, 1024], F32)
                rstd16b = Ta("rstd16b", [16, S], BF16)
                rcp = Ta("rcp", [64, 512], BF16)
                strm = [Ta("strm0", [128, S], F32), Ta("strm1", [128, S], F32)]
                latb = Ta("latb", [128, 2, S], BF16)
                lat = [latb[:, 0, :], latb[:, 1, :]]

                kb.barrier()

                ttr_last = [None, None]
                tcp_prev = [None, None]
                tsq_prev = None
                tst_prev = None
                tmm_prev = [None, None]
                st_toks = []
                for tc in range(NTC):
                    par = tc % 2
                    t_l = kb.dma("sync", "LS%d" % par, strm[par][:, 0:1024],
                                 x_in[s, tc * 128:(tc + 1) * 128, :],
                                 waits=(ttr_last[par],))
                    ttr = []
                    for k in range(NCH):
                        cc = par * 1024 + k * 128
                        ttr.append(trans((cc, cc + 128), A[:, cc:cc + 128],
                                         strm[par][:, k * 128:k * 128 + 128],
                                         waits=(t_l,)))
                    ttr_last[par] = ttr[-1]
                    t_cp = dve(lambda e, p=par: e.tensor_copy(
                        outstage[...], A[:, p * 1024:p * 1024 + 1024]),
                        waits=tuple(ttr) + (tsq_prev, tst_prev),
                        psum_read=(par * 1024, par * 1024 + 1024))
                    t_sq = act(lambda e, p=par: e.activation(
                        sqb[:, p * 1024:p * 1024 + 1024], outstage[...],
                        AF.Square), waits=(t_cp, tmm_prev[par]))
                    tsq_prev = t_sq
                    tm = []
                    for k in range(NCH):
                        tm.append(mm((2048 + tc * 128, 2048 + tc * 128 + 128),
                                     A[0:1, 2048 + tc * 128:2048 + tc * 128 + 128],
                                     ones128[...],
                                     sqb[:, par * 1024 + k * 128:
                                         par * 1024 + k * 128 + 128],
                                     start=(k == 0), stop=(k == NCH - 1),
                                     waits=(t_sq,)))
                    tmm_prev[par] = tm[-1]
                    t_st = kb.dma("scalar", "SOU",
                                  scr0[s][:, :, tc * 128:(tc + 1) * 128]
                                  .transpose([1, 0, 2]),
                                  outstage[...].rearrange("p (k t) -> p k t", k=8),
                                  waits=(t_cp,))
                    tst_prev = t_st
                    st_toks.append(t_st)
                x0_done = st_toks[-1]

                t_bcast = []
                for hh in range(2):
                    t_ln = act(lambda e, hh=hh: e.activation(
                        r1[...], A[0:1, 2048 + hh * 1024:3072 + hh * 1024],
                        AF.Ln, bias=C_EPS_RMS, scale=C_INV_D),
                        waits=(tmm_prev[0], tmm_prev[1]) + tuple(t_bcast),
                        psum_read=(2048 + hh * 1024, 3072 + hh * 1024))
                    t_rms = act(lambda e: e.activation(
                        r2[...], r1[...], AF.Exp, scale=C_NHALF),
                        waits=(t_ln,))
                    t_h = dve(lambda e: e.tensor_copy(rh[...], r2[...]),
                              waits=(t_rms,) + tuple(t_bcast))
                    t_lo = dve(lambda e: e.tensor_tensor(
                        rl[...], r2[...], rh[...], OP.subtract),
                        waits=(t_h,))
                    for p in range(2):
                        cc = hh * 1024 + p * 512
                        mm((cc, cc + 512), A[:, cc:cc + 512], onesk1b[...],
                           rh[:, p * 512:p * 512 + 512],
                           start=True, stop=False, waits=(t_lo,))
                        t_bcast.append(mm((cc, cc + 512), A[:, cc:cc + 512],
                                          onesk1b[...],
                                          rl[:, p * 512:p * 512 + 512],
                                          start=False, stop=True,
                                          waits=(t_lo,)))
                t_xn = [None] * NCH
                prevl = [None, None]
                for k in range(NCH):
                    par = k % 2
                    t_l = kb.dma("sync", "LS%d" % par, strm[par][...],
                                 scr0[s][k], waits=(prevl[par], x0_done))
                    t_xn[k] = dve(lambda e, kk=k, p=par: e.tensor_tensor(
                        xnT[:, kk, :], strm[p][...], A[:, 0:2048], OP.mult),
                        waits=(t_l,) + tuple(t_bcast), psum_read=(0, 2048))
                    prevl[par] = t_xn[k]
                t_xn_all = tuple(t_xn)

                for phase in range(2):
                    is_g = (phase == 0)
                    if not is_g:
                        kb.barrier()
                    foldv = n1t if is_g else n2t

                    tk_kv = cast_weight(wd["gkvd" if is_g else "lwk"], wkvb,
                                        8, LAT, fold=foldv)
                    if is_g:
                        tk_up = cast_weight(wd["gkup"], wvb, 2, LAT,
                                            waits=(tk_kv[-1],))
                        t_lv = kb.dma("sync", "LW0", wstage[:, 0:512]
                                      .rearrange("p (kc n) -> p kc n", kc=2),
                                      wd["gvup"], waits=(tk_up[-1],))
                        tk_vp = [dve(lambda e, j=j: e.tensor_copy(
                            wvb[:, 2 + j, :], wstage[:, j * 256:j * 256 + 256]),
                            waits=(t_lv,)) for j in range(2)]
                    else:
                        tk_vp = cast_weight(wd["lwv"], wvb, 8, LAT, fold=foldv,
                                            waits=(tk_kv[-1],))
                    rope_d = cd["ropeg" if is_g else "ropel"]
                    t_lr = kb.dma("sync", "LW0", wstage[...], rope_d[:, 0, :],
                                  waits=(tk_vp[-1],))
                    t_rc = dve(lambda e: e.tensor_copy(ropeC[...], wstage[...]),
                               waits=(t_lr,))
                    t_lr2 = kb.dma("sync", "LW0", wstage[...], rope_d[:, 1, :],
                                   waits=(t_rc,))
                    t_rs = dve(lambda e: e.tensor_copy(ropeS, wstage[...]),
                               waits=(t_lr2,))
                    wstage_free = t_rs

                    lat_toks = ()
                    if is_g:
                        lt = []
                        for m in range(2):
                            tmm = []
                            for k in range(NCH):
                                for p in range(4):
                                    tmm.append(mm(
                                        (p * 512, p * 512 + 512),
                                        A[:, p * 512:p * 512 + 512],
                                        wkvb[:, k, m * 128:m * 128 + 128],
                                        xnT[:, k, p * 512:p * 512 + 512],
                                        start=(k == 0), stop=(k == NCH - 1),
                                        waits=(tk_kv[k],) + t_xn_all))
                            lt.append(dve(lambda e, m=m: e.tensor_copy(
                                lat[m], A[:, 0:2048]), waits=tuple(tmm),
                                psum_read=(0, 2048)))
                        lat_toks = tuple(lt)

                    def rope_qknorm(chunk_idx, t_evac, dst_fn):
                        t_perm = []
                        for p in range(4):
                            t_perm.append(mm((p * 512, p * 512 + 512),
                                             A[:, p * 512:p * 512 + 512],
                                             perm[...],
                                             qbf[:, p * 512:p * 512 + 512],
                                             start=True, stop=True,
                                             waits=(t_evac, t_rc, t_rs)))
                        t_t1 = dve(lambda e: e.tensor_tensor(
                            t1buf, qbf[...], ropeC[...], OP.mult),
                            waits=(t_evac, t_rc))
                        t_r1 = dve(lambda e: e.tensor_tensor(
                            qrope[...], A[:, 0:2048], ropeS, OP.mult),
                            waits=tuple(t_perm) + (t_rs,),
                            psum_read=(0, 2048))
                        t_r2 = dve(lambda e: e.tensor_tensor(
                            qrope[...], qrope[...], t1buf, OP.add),
                            waits=(t_t1,))
                        t_sq = act(lambda e: e.activation(
                            sqb[:, 0:2048], qrope[...], AF.Square),
                            waits=(t_r2,))
                        t_ssq = []
                        for p in range(4):
                            t_ssq.append(mm(
                                (2048 + p * 512, 2048 + p * 512 + 512),
                                A[0:16, 2048 + p * 512:2048 + p * 512 + 512],
                                hsel[:, chunk_idx, :],
                                sqb[:, p * 512:p * 512 + 512],
                                start=True, stop=True, waits=(t_sq,)))
                        t_exq = None
                        for hh in range(2):
                            t_lnq = act(lambda e, hh=hh: e.activation(
                                rstd16f[...],
                                A[0:16, 2048 + hh * 1024:3072 + hh * 1024],
                                AF.Ln, bias=C_EPS_RMS, scale=C_INV_HD),
                                waits=tuple(t_ssq) +
                                ((t_exq,) if t_exq else ()),
                                psum_read=(2048 + hh * 1024, 3072 + hh * 1024))
                            t_exq = act(lambda e, hh=hh: e.activation(
                                rstd16b[:, hh * 1024:hh * 1024 + 1024],
                                rstd16f[...], AF.Exp, scale=C_NHALF),
                                waits=(t_lnq,))
                        t_bc = []
                        for p in range(4):
                            t_bc.append(mm((p * 512, p * 512 + 512),
                                           A[:, p * 512:p * 512 + 512],
                                           sel16[:, chunk_idx, :],
                                           rstd16b[:, p * 512:p * 512 + 512],
                                           start=True, stop=True,
                                           waits=(t_exq,)))
                        return dst_fn(t_bc, t_r2)

                    k2_toks = []
                    for m in range(2):
                        tmm = []
                        if is_g:
                            for kc in range(2):
                                for p in range(4):
                                    tmm.append(mm(
                                        (p * 512, p * 512 + 512),
                                        A[:, p * 512:p * 512 + 512],
                                        wvb[:, kc, m * 128:m * 128 + 128],
                                        lat[kc][:, p * 512:p * 512 + 512],
                                        start=(kc == 0), stop=(kc == 1),
                                        waits=lat_toks + (tk_up[kc],)))
                        else:
                            for k in range(NCH):
                                for p in range(4):
                                    tmm.append(mm(
                                        (p * 512, p * 512 + 512),
                                        A[:, p * 512:p * 512 + 512],
                                        wkvb[:, k, m * 128:m * 128 + 128],
                                        xnT[:, k, p * 512:p * 512 + 512],
                                        start=(k == 0), stop=(k == NCH - 1),
                                        waits=(tk_kv[k],) + t_xn_all))
                        t_evac = dve(lambda e: e.tensor_copy(qbf[...],
                                                             A[:, 0:2048]),
                                     waits=tuple(tmm), psum_read=(0, 2048))

                        def k_dst(t_bc, t_r2, m=m):
                            toks = []
                            for hloc in range(2):
                                g = 2 * m + hloc
                                r0 = hloc * 64
                                tt = dve(lambda e, g=g, r0=r0:
                                         e.tensor_tensor(
                                             k2[r0:r0 + 64, g, :],
                                             qrope[r0:r0 + 64, :],
                                             A[r0:r0 + 64, 0:2048],
                                             OP.mult),
                                         waits=tuple(t_bc) + (t_r2,),
                                         psum_read=(0, 2048))
                                d0 = 64 - r0
                                tcp = kb.dma("gpsimd", "KD",
                                             k2[d0:d0 + 64, g, :],
                                             k2[r0:r0 + 64, g, :],
                                             waits=(tt,))
                                toks.append(tcp)
                            return toks
                        k2_toks += rope_qknorm(m, t_evac, k_dst)
                    k2_all = tuple(k2_toks)

                    if debug and s == 0 and is_g:
                        kb.dma("gpsimd", "KD", dbg_k[...], k2[...],
                               waits=tuple(k2_toks))
                    wq_d = wd["gwq" if is_g else "lwq"]
                    wo_d = wd["gwo" if is_g else "lwo"]
                    wq_cast_tok = [wstage_free, wstage_free]
                    ot_toks = []
                    qfin_tok = [[None] * 4 for _ in range(NCH)]
                    tw_by_qc = [None] * NCH
                    rot = [0]
                    rot_mod = [3]

                    def take_slot():
                        w = rot[0] % rot_mod[0]
                        rot[0] += 1
                        return w

                    def wq_cast(qci, foldv=foldv):
                        par = qci % 2
                        t_lw = kb.dma("sync", "LW%d" % par,
                                      wstage[:, 1024 * par:1024 * par + 1024]
                                      .rearrange("p (kc n) -> p kc n", kc=8),
                                      wq_d[qci], waits=(wq_cast_tok[par],))
                        tw = []
                        for j in range(NCH):
                            tw.append(dve(lambda e, j=j, par=par,
                                          fv=foldv[:, j:j + 1]:
                                          e.tensor_scalar(
                                              wqb[:, par, j, :],
                                              wstage[:, 1024 * par + j * 128:
                                                      1024 * par + j * 128
                                                      + 128],
                                              fv, None, OP.mult),
                                          waits=(t_lw,)))
                        tw_by_qc[qci] = tw

                    def mk_chain(qci, pan):
                        lo = pan * 512
                        par = qci % 2
                        cs_ = {}

                        def th_qmm(lo=lo, par=par, qci=qci, cs_=cs_):
                            w = take_slot()
                            cs = w * 1024
                            tmm = []
                            for k in range(NCH):
                                tmm.append(mm(
                                    (cs, cs + 512), A[:, cs:cs + 512],
                                    wqb[:, par, k, :],
                                    xnT[:, k, lo:lo + 512],
                                    start=(k == 0), stop=(k == NCH - 1),
                                    waits=(tw_by_qc[qci][k],) + t_xn_all))
                            wq_cast_tok[par] = tmm[-1]
                            cs_["evac"] = dve(
                                lambda e, cs=cs, lo=lo: e.tensor_copy(
                                    qbf[:, lo:lo + 512], A[:, cs:cs + 512]),
                                waits=(tmm[-1],), psum_read=(cs, cs + 512))

                        def th_perm(lo=lo, cs_=cs_):
                            w = take_slot()
                            cs = w * 1024
                            t_p = mm((cs, cs + 512), A[:, cs:cs + 512],
                                     perm[...], qbf[:, lo:lo + 512],
                                     start=True, stop=True,
                                     waits=(cs_["evac"], t_rc, t_rs))
                            t_t1 = dve(lambda e, lo=lo: e.tensor_tensor(
                                sqb[:, lo:lo + 512], qbf[:, lo:lo + 512],
                                ropeC[:, lo:lo + 512], OP.mult),
                                waits=(cs_["evac"], t_rc))
                            t_r1 = dve(lambda e, cs=cs, lo=lo:
                                       e.tensor_tensor(
                                           qrope[:, lo:lo + 512],
                                           A[:, cs:cs + 512],
                                           ropeS[:, lo:lo + 512], OP.mult),
                                       waits=(t_p, t_rs),
                                       psum_read=(cs, cs + 512))
                            t_r2 = dve(lambda e, lo=lo: e.tensor_tensor(
                                qrope[:, lo:lo + 512], qrope[:, lo:lo + 512],
                                sqb[:, lo:lo + 512], OP.add),
                                waits=(t_t1, t_r1))
                            cs_["sq"] = dve(lambda e, lo=lo: e.tensor_tensor(
                                sqb[:, lo:lo + 512], qrope[:, lo:lo + 512],
                                qrope[:, lo:lo + 512], OP.mult),
                                waits=(t_r2,))

                        def th_ssq(lo=lo, qci=qci, cs_=cs_):
                            w = take_slot()
                            cs = w * 1024
                            t_s = mm((cs, cs + 512), A[0:16, cs:cs + 512],
                                     hsel[:, qci, :], sqb[:, lo:lo + 512],
                                     start=True, stop=True,
                                     waits=(cs_["sq"],))
                            t_ln = act(lambda e, cs=cs: e.activation(
                                rstd16f[:, 0:512], A[0:16, cs:cs + 512],
                                AF.Ln, bias=C_EPS_RMS, scale=C_INV_HD),
                                waits=(t_s,), psum_read=(cs, cs + 512))
                            cs_["exq"] = act(lambda e, lo=lo: e.activation(
                                rstd16b[:, lo:lo + 512], rstd16f[:, 0:512],
                                AF.Exp, scale=C_NHALF), waits=(t_ln,))

                        def th_bc(lo=lo, qci=qci, pan=pan, cs_=cs_):
                            w = take_slot()
                            cs = w * 1024
                            t_b = mm((cs, cs + 512), A[:, cs:cs + 512],
                                     sel16[:, qci, :],
                                     rstd16b[:, lo:lo + 512],
                                     start=True, stop=True,
                                     waits=(cs_["exq"],))
                            qfin_tok[qci][pan] = dve(
                                lambda e, cs=cs, lo=lo: e.tensor_tensor(
                                    qfin[:, lo:lo + 512],
                                    qrope[:, lo:lo + 512],
                                    A[:, cs:cs + 512], OP.mult),
                                waits=(t_b,), psum_read=(cs, cs + 512))

                        return [th_qmm, th_perm, th_ssq, th_bc]

                    wq_cast(0)
                    rot_mod[0] = 2
                    ch0 = []
                    for pan0 in range(4):
                        ch0 += mk_chain(0, pan0)
                    v_toks = []
                    for tc in range(NTC):
                        par = tc % 2
                        c0 = 2048 + par * 512
                        reg = A[:, c0:c0 + 256]
                        tmm = []
                        if is_g:
                            for kc in range(2):
                                tmm.append(mm((c0, c0 + 256), reg,
                                              lat[kc][:, tc * 128:tc * 128 + 128],
                                              wvb[:, 2 + kc, :],
                                              start=(kc == 0), stop=(kc == 1),
                                              waits=lat_toks + (tk_vp[-1],)))
                        else:
                            for k in range(NCH):
                                tmm.append(mm((c0, c0 + 256), reg,
                                              xnT[:, k, tc * 128:tc * 128 + 128],
                                              wvb[:, k, :],
                                              start=(k == 0), stop=(k == NCH - 1),
                                              waits=(tk_vp[k],) + t_xn_all))
                        v_toks.append(dve(lambda e, tc=tc, r=reg: e.tensor_copy(
                            v_all[:, tc, :, :], r), waits=tuple(tmm),
                            psum_read=(c0, c0 + 256)))
                        ch0[tc]()
                    v_all_tok = (v_toks[-1], v_toks[-2])
                    rot_mod[0] = 3

                    for qc in range(NCH):
                        par = qc % 2
                        g = qc // 2
                        if qc + 1 < NCH:
                            wq_cast(qc + 1)
                        if debug and s == 0 and is_g and qc == 0:
                            kb.dma("gpsimd", "KD", dbg_q[...], qfin[...],
                                   waits=tuple(qfin_tok[0]))

                        for pan in range(4):
                            lo = pan * 512
                            hi = lo + 512
                            t_qfin = qfin_tok[qc][pan]
                            ch = (mk_chain(qc + 1, pan)
                                  if qc + 1 < NCH else None)
                            if is_g:
                                chunks = list(range(0, hi // 128))
                            else:
                                chunks = [c for c in range(NTC)
                                          if c * 128 < hi
                                          and (c + 3) * 128 > lo]
                            av_zero = []
                            if not is_g:
                                av_zero.append(mm(
                                    (3072, 3584), A[0:64, 3072:3584],
                                    zl64, zr512, start=True, stop=True,
                                    skip=True, tile_position=(0, 0)))
                                av_zero.append(mm(
                                    (3072, 3584), A[64:128, 3072:3584],
                                    zl64, zr512, start=True, stop=True,
                                    skip=True, tile_position=(0, 64)))
                                av_zero.append(mm(
                                    (3584, 4096), A[0:32, 3584:4096],
                                    zl32, zr512, start=True, stop=True,
                                    skip=True, tile_position=(0, 0)))
                                av_zero.append(mm(
                                    (3584, 4096), A[32:64, 3584:4096],
                                    zl32, zr512, start=True, stop=True,
                                    skip=True, tile_position=(0, 32)))
                            pieces = []
                            for c in chunks:
                                a = max(c * 128, lo)
                                b = hi if is_g else min((c + 3) * 128, hi)
                                pieces.append((c, a, b))
                            av_toks = []
                            rs_toks = []
                            pfree = [None, None, None]
                            pend_av = [None, None, None]

                            piece_slot = {}

                            def emit_scores(i):
                                c, a, b = pieces[i]
                                wA = take_slot()
                                piece_slot[i] = wA
                                wdt = b - a
                                cs = wA * 1024
                                t_sa = mm((cs, cs + wdt), A[:, cs:cs + wdt],
                                          k2[0:64, g, c * 128:c * 128 + 128],
                                          qfin[0:64, a:b],
                                          start=True, stop=True,
                                          waits=(t_qfin,) + k2_all)
                                t_sb = mm((cs + 512, cs + 512 + wdt),
                                          A[:, cs + 512:cs + 512 + wdt],
                                          k2[64:128, g, c * 128:c * 128 + 128],
                                          qfin[64:128, a:b],
                                          start=True, stop=True,
                                          waits=(t_qfin,) + k2_all)
                                t_e = act(lambda e, wA=wA, wdt=wdt, cs=cs:
                                          e.activation(
                                              pab[:, 2 * wA:2 * wA + 2, 0:wdt],
                                              A[:, cs:cs + 1024]
                                              .rearrange("p (s w) -> p s w", s=2)
                                              [:, :, 0:wdt],
                                              AF.Exp, bias=C_EXP_BIAS,
                                              scale=C_EXP_SCALE),
                                          waits=(t_sa, t_sb, pfree[wA]),
                                          psum_read=(cs, cs + 1024))
                                mtoks = []
                                mask_blocks = ((c, 0),) if is_g                                     else ((c, 0), (c + 2, 1))
                                for (blk, mi) in mask_blocks:
                                    bstart = blk * 128
                                    if bstart < a or bstart >= b:
                                        continue
                                    off = bstart - a
                                    for hslot in range(2):
                                        mtoks.append(dve(
                                            lambda e, hs=hslot, wA=wA,
                                                   off=off, mi=mi:
                                            e.tensor_tensor(
                                                pab[:, 2 * wA + hs,
                                                    off:off + 128],
                                                pab[:, 2 * wA + hs,
                                                    off:off + 128],
                                                masks[:, mi, :], OP.mult),
                                            waits=(t_e,)))
                                pend_av[wA] = (i, tuple(mtoks) if mtoks
                                               else (t_e,))

                            def emit_av(wA):
                                if pend_av[wA] is None:
                                    return
                                i, avw0 = pend_av[wA]
                                pend_av[wA] = None
                                c, a, b = pieces[i]
                                wdt = b - a
                                la = a - lo
                                lb = b - lo
                                avw = avw0 + v_all_tok + tuple(av_zero)
                                fresh = is_g and (c == 0)
                                is_last_c = (c == chunks[-1])
                                t1 = mm((3072 + la, 3072 + lb),
                                        A[0:64, 3072 + la:3072 + lb],
                                        v_all[:, c, g, :],
                                        pab[:, 2 * wA, 0:wdt],
                                        start=fresh, stop=is_last_c,
                                        skip=True, waits=avw,
                                        tile_position=(0, 0))
                                t2 = mm((3072 + la, 3072 + lb),
                                        A[64:128, 3072 + la:3072 + lb],
                                        v_all[:, c, g, :],
                                        pab[:, 2 * wA + 1, 0:wdt],
                                        start=fresh, stop=is_last_c,
                                        skip=True, waits=avw,
                                        tile_position=(0, 64))
                                t3 = mm((3584 + la, 3584 + lb),
                                        A[0:32, 3584 + la:3584 + lb],
                                        onesM[...], pab[:, 2 * wA, 0:wdt],
                                        start=fresh, stop=is_last_c,
                                        skip=True, waits=avw,
                                        tile_position=(0, 0))
                                t4 = mm((3584 + la, 3584 + lb),
                                        A[32:64, 3584 + la:3584 + lb],
                                        onesM[...],
                                        pab[:, 2 * wA + 1, 0:wdt],
                                        start=fresh, stop=is_last_c,
                                        skip=True, waits=avw,
                                        tile_position=(0, 32))
                                av_toks.extend((t1, t2))
                                rs_toks.extend((t3, t4))
                                pfree[wA] = t4

                            n_p = len(pieces)
                            ch_pos = {0: 0, 1: 1, 3: 2}
                            for i in range(n_p):
                                if i >= 2:
                                    emit_av(piece_slot[i - 2])
                                emit_scores(i)
                                if ch is not None and i in ch_pos:
                                    ch[ch_pos[i]]()
                            emit_av(piece_slot[n_p - 2])
                            emit_av(piece_slot[n_p - 1])
                            if ch is not None:
                                ch[3]()

                            t_l = act(lambda e: e.activation(
                                rcp[...], A[0:64, 3584:4096], AF.Ln),
                                waits=tuple(rs_toks[-2:]),
                                psum_read=(3584, 4096))
                            t_r = act(lambda e: e.activation(
                                rcp[...], rcp[...], AF.Exp, scale=-1.0),
                                waits=(t_l,))
                            t_nb = mm((2560, 3072), A[:, 2560:3072],
                                      sel2[...], rcp[...], start=True,
                                      stop=True, waits=(t_r,))
                            t_cp = dve(lambda e, qc=qc, lo=lo: e.tensor_copy(
                                OT[:, qc, lo:lo + 512], A[:, 3072:3584]),
                                waits=tuple(av_toks[-2:]),
                                psum_read=(3072, 3584))
                            t_mul = dve(lambda e, qc=qc, lo=lo:
                                        e.tensor_tensor(
                                            OT[:, qc, lo:lo + 512],
                                            OT[:, qc, lo:lo + 512],
                                            A[:, 2560:3072], OP.mult),
                                        waits=(t_nb, t_cp),
                                        psum_read=(2560, 3072))
                            ot_toks.append(t_mul)

                    res_src = scr0[s] if is_g else scrA
                    res_dst = scrA if is_g else scrB
                    src_gate = x0_done if is_g else res_done_g
                    wo_cast_tok = [wq_cast_tok[0], wq_cast_tok[1]]
                    t_res_st = []
                    prev_add = None
                    prev_sq = None
                    prev_ld = [None, None]
                    red_toks = []
                    for m in range(NCH):
                        par = m % 2
                        t_lw = kb.dma("sync", "LW%d" % par,
                                      wstage[:, 1024 * par:1024 * par + 1024]
                                      .rearrange("p (kc n) -> p kc n", kc=8),
                                      wo_d[m], waits=(wo_cast_tok[par],))
                        tw = []
                        for j in range(NCH):
                            tw.append(act(lambda e, j=j, par=par:
                                          e.activation(
                                              wob[:, par, j, :],
                                              wstage[:, 1024 * par + j * 128:
                                                      1024 * par + j * 128
                                                      + 128],
                                              AF.Copy),
                                          waits=(t_lw,)))
                        wo_cast_tok[par] = tw[-1]
                        tmm = []
                        for k in range(NCH):
                            for p in range(4):
                                tmm.append(mm(
                                    (p * 512, p * 512 + 512),
                                    A[:, p * 512:p * 512 + 512],
                                    wob[:, par, k, :],
                                    OT[:, k, p * 512:p * 512 + 512],
                                    start=(k == 0), stop=(k == NCH - 1),
                                    waits=(tw[k], ot_toks[-1])))
                        sp = par if is_g else 0
                        t_l1 = kb.dma("sync", "LS%d" % sp, strm[sp][...],
                                      res_src[m], waits=(prev_ld[sp],
                                                         src_gate))
                        wr = (t_l1, prev_sq) + tuple(tmm)
                        if t_res_st:
                            wr = wr + (t_res_st[-1],)
                        t_l2 = None
                        if not is_g:
                            t_l2 = kb.dma("sync", "LS1", strm[1][...],
                                          scr0[s][m],
                                          waits=(prev_ld[1], x0_done))
                        t_a = dve(lambda e, sp=sp: e.tensor_tensor(
                            qrope[...], A[:, 0:2048], strm[sp][...],
                            OP.add), waits=wr, psum_read=(0, 2048))
                        prev_ld[sp] = t_a
                        if not is_g:
                            t_a = dve(lambda e: e.tensor_tensor(
                                qrope[...], qrope[...], strm[1][...],
                                OP.add), waits=(t_a, t_l2))
                            prev_ld[1] = t_a
                        prev_add = t_a
                        sb = (m % 2) * 2048
                        t_sq = act(lambda e, sb=sb: e.activation(
                            sqb[:, sb:sb + 2048], qrope[...], AF.Square),
                            waits=(t_a,))
                        prev_sq = t_sq
                        rr = []
                        for p in range(4):
                            rr.append(mm(
                                (2048 + p * 512, 2048 + p * 512 + 512),
                                A[0:1, 2048 + p * 512:2048 + p * 512 + 512],
                                ones128[...],
                                sqb[:, sb + p * 512:sb + p * 512 + 512],
                                start=(m == 0), stop=(m == NCH - 1),
                                skip=True, waits=(t_sq,),
                                tile_position=(0, 0)))
                        if not is_g:
                            t_cb = act(lambda e, par=par: e.activation(
                                latb[:, par, :], qrope[...], AF.Copy),
                                waits=(t_a,))
                            prev_add = t_cb
                            prev_sq = t_cb
                            for p in range(4):
                                rr.append(mm(
                                    (2048 + p * 512,
                                     2048 + p * 512 + 512),
                                    A[32:33,
                                      2048 + p * 512:2048 + p * 512 + 512],
                                    ones128[...],
                                    latb[:, par, p * 512:p * 512 + 512],
                                    start=(m == 0), stop=(m == NCH - 1),
                                    skip=True, waits=(t_cb,),
                                    tile_position=(0, 32)))
                        red_toks = rr
                        t_st = kb.dma("scalar", "SRES", res_dst[m],
                                      qrope[...], waits=(prev_add,))
                        t_res_st.append(t_st)
                    res_done = t_res_st[-1]
                    if is_g:
                        res_done_g = res_done
                        red_toks_g = red_toks
                    else:
                        red_toks_l = red_toks
                        res_st_l = list(t_res_st)

                    if is_g:
                        t_bcast = []
                        for hh in range(2):
                            t_ln = act(lambda e, hh=hh: e.activation(
                                r1[...],
                                A[0:1, 2048 + hh * 1024:3072 + hh * 1024],
                                AF.Ln, bias=C_EPS_RMS, scale=C_INV_D),
                                waits=tuple(red_toks_g) + tuple(t_bcast),
                                psum_read=(2048 + hh * 1024,
                                           3072 + hh * 1024))
                            t_rms = act(lambda e: e.activation(
                                r2[...], r1[...], AF.Exp, scale=C_NHALF),
                                waits=(t_ln,))
                            t_h = dve(lambda e: e.tensor_copy(
                                rh[...], r2[...]),
                                waits=(t_rms,) + tuple(t_bcast))
                            t_lo = dve(lambda e: e.tensor_tensor(
                                rl[...], r2[...], rh[...], OP.subtract),
                                waits=(t_h,))
                            for p in range(2):
                                cc = hh * 1024 + p * 512
                                mm((cc, cc + 512), A[:, cc:cc + 512],
                                   onesk1b[...],
                                   rh[:, p * 512:p * 512 + 512],
                                   start=True, stop=False, waits=(t_lo,))
                                t_bcast.append(mm((cc, cc + 512),
                                                  A[:, cc:cc + 512],
                                                  onesk1b[...],
                                                  rl[:, p * 512:p * 512 + 512],
                                                  start=False, stop=True,
                                                  waits=(t_lo,)))
                        t_xn = [None] * NCH
                        prevl = [prev_ld[0], prev_ld[1]]
                        for k in range(NCH):
                            par = k % 2
                            t_l = kb.dma("sync", "LS%d" % par,
                                         strm[par][...], scrA[k],
                                         waits=(prevl[par], t_res_st[k]))
                            t_xn[k] = dve(lambda e, kk=k, p=par:
                                          e.tensor_tensor(
                                              xnT[:, kk, :], strm[p][...],
                                              A[:, 0:2048], OP.mult),
                                          waits=(t_l,) + tuple(t_bcast),
                                          psum_read=(0, 2048))
                            prevl[par] = t_xn[k]
                        t_xn_all = tuple(t_xn)

                kb.flush()


            fst = ExitStack()
            with fst:
                Tf = lambda n, *a: fst.enter_context(nc.sbuf_tensor(n + "f%d" % s, *a))
                w2b = Tf("w2b", [128, 32, 1024], BF16)
                hT = Tf("hT", [128, 32, 512], BF16)
                w1mb = Tf("w1mb", [128, 2, NCH, 128], BF16)
                zpan = Tf("zpan", [128, NCH, 512], F32)
                zbf = hT[:, 24:32, :]
                f2048 = zpan[:, 0:4, :].rearrange("p a b -> p (a b)")
                strmF = [Tf("strmF0", [128, S], F32),
                         Tf("strmF1", [128, S], F32)]

                kb.barrier()

                t_stats = red_toks_l
                t_ab = []
                t_bb = []
                for hh in range(2):
                    h0 = hh * 1024
                    t_mean = act(lambda e, h0=h0: e.activation(
                        r1[...], A[32:33, 2048 + h0:2048 + h0 + 1024],
                        AF.Copy, scale=C_INV_D),
                        waits=tuple(t_stats) + tuple(t_ab) + tuple(t_bb),
                        psum_read=(2048 + h0, 2048 + h0 + 1024))
                    t_msq = act(lambda e: e.activation(
                        r2[...], r1[...], AF.Square), waits=(t_mean,))
                    t_v1 = dve(lambda e, h0=h0: e.tensor_scalar(
                        r3[...], A[0:1, 2048 + h0:2048 + h0 + 1024],
                        C_INV_D, None, OP.mult),
                        waits=tuple(t_stats) + tuple(t_bb),
                        psum_read=(2048 + h0, 2048 + h0 + 1024))
                    t_v2 = dve(lambda e: e.tensor_tensor(
                        r3[...], r3[...], r2[...], OP.subtract),
                        waits=(t_v1, t_msq))
                    t_lnr = act(lambda e: e.activation(
                        r2[...], r3[...], AF.Ln, bias=C_EPS_LN),
                        waits=(t_v2,))
                    t_a = act(lambda e: e.activation(
                        r2[...], r2[...], AF.Exp, scale=C_NHALF),
                        waits=(t_lnr,))
                    t_b1_ = dve(lambda e: e.tensor_tensor(
                        r3[...], r1[...], r2[...], OP.mult), waits=(t_a,))
                    t_b = dve(lambda e: e.tensor_scalar(
                        r3[...], r3[...], -1.0, None, OP.mult),
                        waits=(t_b1_,))
                    t_h = dve(lambda e: e.tensor_copy(rh[...], r2[...]),
                              waits=(t_a,) + tuple(t_ab))
                    t_lo = dve(lambda e: e.tensor_tensor(
                        rl[...], r2[...], rh[...], OP.subtract), waits=(t_h,))
                    for p in range(2):
                        cc = h0 + p * 512
                        mm((cc, cc + 512), A[:, cc:cc + 512], onesk1b[...],
                           rh[:, p * 512:p * 512 + 512],
                           start=True, stop=False, waits=(t_lo,))
                        t_ab.append(mm((cc, cc + 512), A[:, cc:cc + 512],
                                       onesk1b[...],
                                       rl[:, p * 512:p * 512 + 512],
                                       start=False, stop=True, waits=(t_lo,)))
                    t_h2 = dve(lambda e: e.tensor_copy(rh[...], r3[...]),
                               waits=(t_b,) + tuple(t_ab) + tuple(t_bb))
                    t_lo2 = dve(lambda e: e.tensor_tensor(
                        rl[...], r3[...], rh[...], OP.subtract), waits=(t_h2,))
                    for p in range(2):
                        cc = h0 + p * 512
                        mm((2048 + cc, 2048 + cc + 512),
                           A[:, 2048 + cc:2048 + cc + 512], onesk1b[...],
                           rh[:, p * 512:p * 512 + 512],
                           start=True, stop=False, waits=(t_lo2,))
                        t_bb.append(mm((2048 + cc, 2048 + cc + 512),
                                       A[:, 2048 + cc:2048 + cc + 512],
                                       onesk1b[...],
                                       rl[:, p * 512:p * 512 + 512],
                                       start=False, stop=True,
                                       waits=(t_lo2,)))
                zstg = zpan[:, 4:8, :].rearrange("p a b -> p (a b)")
                stg3 = [strmF[0][...], strmF[1][...], zstg]
                t_xnf = []
                prev_t1p = [None, None, None]
                for k in range(NCH):
                    par = k % 3
                    t_l = kb.dma("sync", "LSF", stg3[par], scrB[k],
                                 waits=(prev_t1p[par],
                                        strmF_guard[par] if par < 2 else None,
                                        res_st_l[k]))
                    t1 = dve(lambda e, p=par: e.tensor_tensor(
                        f2048, stg3[p], A[:, 0:2048], OP.mult),
                        waits=(t_l,) + tuple(t_ab) +
                        ((t_xnf[-1],) if t_xnf else ()),
                        psum_read=(0, 2048))
                    t2 = dve(lambda e: e.tensor_tensor(
                        f2048, f2048, A[:, 2048:4096], OP.add),
                        waits=tuple(t_bb), psum_read=(2048, 4096))
                    t3 = act(lambda e, kk=k: e.activation(
                        xnT[:, kk, :], f2048, AF.Copy), waits=(t2,))
                    t_xnf.append(t3)
                    prev_t1p[par] = t1
                ap_last = [prev_t1p[0], prev_t1p[1]]

                prev_cp2 = [None, None]
                prev_cpf = [strmF_guard[0], strmF_guard[1]]
                w2_toks = []
                for piece in range(32):
                    pr = piece % 2
                    if piece < 12:
                        stg = wstage[:, pr * 1024:pr * 1024 + 1024]
                        t_l = kb.dma("gpsimd", "W2L%d" % pr, stg,
                                     wd["w2c"][:, piece, :],
                                     waits=(prev_cp2[pr],))
                        tcp = pool(lambda e, pc=piece, stg=stg:
                                   e.tensor_copy(w2b[:, pc, :], stg),
                                   waits=(t_l,))
                        prev_cp2[pr] = tcp
                    else:
                        stg = strmF[pr][:, 0:1024]
                        t_l = kb.dma("gpsimd", "W2L%d" % pr, stg,
                                     wd["w2c"][:, piece, :],
                                     waits=(prev_cpf[pr], ap_last[pr]))
                        tcp = pool(lambda e, pc=piece, stg=stg:
                                   e.tensor_copy(w2b[:, pc, :], stg),
                                   waits=(t_l,))
                        prev_cpf[pr] = tcp
                    w2_toks.append(tcp)
                strmF_guard = [prev_cpf[0], prev_cpf[1]]

                w1_cast_tok = [w2_toks[10], w2_toks[11]]
                tail_hooks = {}
                tl_prev = None
                prev_cp = None
                prev_st = None

                for pan in range(4):
                    plo = pan * 512
                    gel_toks = []
                    for mdf in range(32):
                        for th in tail_hooks.pop(mdf, ()):
                            th()
                        par = mdf % 2
                        t_lw = kb.dma("sync", "LW%d" % par,
                                      wstage[:, 1024 * par:1024 * par + 1024]
                                      .rearrange("p (kc n) -> p kc n", kc=8),
                                      wd["w1c"][mdf], waits=(w1_cast_tok[par],))
                        t_c = dve(lambda e, par=par: e.tensor_copy(
                            w1mb[:, par, :, :],
                            wstage[:, 1024 * par:1024 * par + 1024]
                            .rearrange("p (kc n) -> p kc n", kc=8)),
                            waits=(t_lw,))
                        c0 = (mdf % 4) * 512
                        tmm = []
                        for k in range(NCH):
                            tmm.append(mm((c0, c0 + 512),
                                          A[:, c0:c0 + 512],
                                          w1mb[:, par, k, :],
                                          xnT[:, k, plo:plo + 512],
                                          start=(k == 0), stop=(k == NCH - 1),
                                          waits=(t_c, t_xnf[k])))
                        w1_cast_tok[par] = tmm[-1]
                        ggate = ((tl_prev["tsm"],)
                                 if (tl_prev is not None and mdf >= 24)
                                 else ())
                        gel_toks.append(act(
                            lambda e, mdf=mdf, c0=c0: e.activation(
                                hT[:, mdf, :], A[:, c0:c0 + 512],
                                AF.Gelu_apprx_tanh, bias=b1t[:, mdf:mdf + 1]),
                            waits=tuple(tmm) + ggate,
                            psum_read=(c0, c0 + 512)))
                    for kk in sorted(tail_hooks):
                        for th in tail_hooks.pop(kk):
                            th()
                    ztoks = []
                    for m in range(NCH):
                        tmmY = []
                        for k in range(32):
                            tmmY.append(mm((m * 512, m * 512 + 512),
                                           A[:, m * 512:m * 512 + 512],
                                           w2b[:, k, m * 128:m * 128 + 128],
                                           hT[:, k, :],
                                           start=(k == 0), stop=(k == 31),
                                           waits=(w2_toks[k],
                                                  gel_toks[k])))
                        t_e = dve(lambda e, m=m: e.tensor_scalar(
                            zpan[:, m, :], A[:, m * 512:m * 512 + 512],
                            b2t[:, m:m + 1], None, OP.add),
                            waits=(tmmY[-1],),
                            psum_read=(m * 512, m * 512 + 512))
                        t_z = dve(lambda e, m=m, plo=plo: e.tensor_tensor(
                            zpan[:, m, :], zpan[:, m, :],
                            xnT[:, m, plo:plo + 512],
                            OP.add), waits=(t_e, t_xnf[m]))
                        ztoks.append(t_z)
                    zb_h = [dve(lambda e, h=h: e.tensor_copy(
                        zbf[:, 4 * h:4 * h + 4, :],
                        zpan[:, 4 * h:4 * h + 4, :]),
                        waits=tuple(ztoks[4 * h:4 * h + 4]))
                        for h in range(2)]
                    zq_h = [act(lambda e, h=h: e.activation(
                        sqb[:, 2048 * h:2048 * h + 2048],
                        zpan[:, 4 * h:4 * h + 4, :]
                        .rearrange("p a b -> p (a b)"),
                        AF.Square), waits=tuple(ztoks[4 * h:4 * h + 4]))
                        for h in range(2)]
                    t_zb = zb_h
                    t_zq = zq_h

                    tl = {}

                    def th_reduce(t_zb=t_zb, t_zq=t_zq, tl=tl):
                        tsm = []
                        tsq = []
                        for m in range(NCH):
                            tsm.append(mm((2048, 2560), A[0:1, 2048:2560],
                                          ones128[...],
                                          zbf[:, m, :], start=(m == 0),
                                          stop=(m == NCH - 1),
                                          waits=(t_zb[m // 4],)))
                            tsq.append(mm((2560, 3072), A[0:1, 2560:3072],
                                          ones128[...],
                                          sqb[:, m * 512:m * 512 + 512],
                                          start=(m == 0),
                                          stop=(m == NCH - 1),
                                          waits=(t_zq[m // 4],)))
                        tl["tsm"] = tsm[-1]
                        tl["tsq"] = tsq[-1]

                    def th_chain(tl=tl):
                        t_mean = act(lambda e: e.activation(
                            r1[:, 0:512], A[0:1, 2048:2560], AF.Copy,
                            scale=C_INV_D), waits=(tl["tsm"],),
                            psum_read=(2048, 2560))
                        t_msq = act(lambda e: e.activation(
                            r2[:, 0:512], r1[:, 0:512], AF.Square),
                            waits=(t_mean,))
                        t_v1 = dve(lambda e: e.tensor_scalar(
                            r3[:, 0:512], A[0:1, 2560:3072], C_INV_D, None,
                            OP.mult), waits=(tl["tsq"],),
                            psum_read=(2560, 3072))
                        t_v2 = dve(lambda e: e.tensor_tensor(
                            r3[:, 0:512], r3[:, 0:512], r2[:, 0:512],
                            OP.subtract), waits=(t_v1, t_msq))
                        t_lnr = act(lambda e: e.activation(
                            r2[:, 0:512], r3[:, 0:512], AF.Ln,
                            bias=C_EPS_LN), waits=(t_v2,))
                        t_a = act(lambda e: e.activation(
                            r2[:, 0:512], r2[:, 0:512], AF.Exp,
                            scale=C_NHALF), waits=(t_lnr,))
                        t_b1_ = dve(lambda e: e.tensor_tensor(
                            r3[:, 0:512], r1[:, 0:512], r2[:, 0:512],
                            OP.mult), waits=(t_a,))
                        t_b = dve(lambda e: e.tensor_scalar(
                            r3[:, 0:512], r3[:, 0:512], -1.0,
                            None, OP.mult), waits=(t_b1_,))
                        t_h = dve(lambda e: e.tensor_copy(
                            rh[:, 0:512], r2[:, 0:512]), waits=(t_a,))
                        t_lo = dve(lambda e: e.tensor_tensor(
                            rl[:, 0:512], r2[:, 0:512], rh[:, 0:512],
                            OP.subtract), waits=(t_h,))
                        tl["t_lo"] = t_lo
                        tl["t_b"] = t_b

                    def th_bcast(tl=tl):
                        mm((2048, 2560), A[:, 2048:2560], onesk1b[...],
                           rh[:, 0:512], start=True, stop=False,
                           waits=(tl["t_lo"],))
                        t_abm = mm((2048, 2560), A[:, 2048:2560],
                                   onesk1b[...], rl[:, 0:512],
                                   start=False, stop=True,
                                   waits=(tl["t_lo"],))
                        t_h2 = dve(lambda e: e.tensor_copy(
                            rh[:, 512:1024], r3[:, 0:512]),
                            waits=(tl["t_b"], t_abm))
                        t_lo2 = dve(lambda e: e.tensor_tensor(
                            rl[:, 512:1024], r3[:, 0:512], rh[:, 512:1024],
                            OP.subtract), waits=(t_h2,))
                        mm((2560, 3072), A[:, 2560:3072], onesk1b[...],
                           rh[:, 512:1024], start=True, stop=False,
                           waits=(t_lo2,))
                        t_bbm = mm((2560, 3072), A[:, 2560:3072],
                                   onesk1b[...], rl[:, 512:1024],
                                   start=False, stop=True, waits=(t_lo2,))
                        tl["t_abm"] = t_abm
                        tl["t_bbm"] = t_bbm

                    def th_apply(t_zb=t_zb, t_zq=t_zq, tl=tl):
                        ap_toks = []
                        for m in range(NCH):
                            t1 = dve(lambda e, m=m: e.tensor_tensor(
                                zpan[:, m, :], zpan[:, m, :],
                                A[:, 2048:2560],
                                OP.mult),
                                waits=(tl["t_abm"], t_zb[1], t_zq[1]),
                                psum_read=(2048, 2560))
                            t2 = dve(lambda e, m=m: e.tensor_tensor(
                                zpan[:, m, :], zpan[:, m, :],
                                A[:, 2560:3072],
                                OP.add), waits=(tl["t_bbm"],),
                                psum_read=(2560, 3072))
                            ap_toks.append(t2)
                        tl["ap"] = tuple(ap_toks)

                    def mk_trans(tb, s=s, plo=plo, tl=tl):
                        def th_trans():
                            nonlocal prev_cp, prev_st
                            preg = 3072
                            ttr = []
                            for k in range(NCH):
                                cc = preg + k * 128
                                ttr.append(trans((cc, cc + 128),
                                                 A[:, cc:cc + 128],
                                                 zpan[:, k,
                                                      tb * 128:tb * 128 + 128],
                                                 waits=tl["ap"]))
                            t_cp = dve(lambda e, preg=preg: e.tensor_copy(
                                outstage[...], A[:, preg:preg + 1024]),
                                waits=tuple(ttr) + (prev_st,),
                                psum_read=(preg, preg + 1024))
                            t_stt = kb.dma("scalar", "SOU",
                                           y_out[s, plo + tb * 128:
                                                 plo + tb * 128 + 128, :],
                                           outstage[...], waits=(t_cp,))
                            prev_st = t_stt
                            prev_cp = t_cp
                        return th_trans

                    hooks = {1: [th_reduce], 3: [th_chain], 9: [th_bcast],
                             11: [th_apply], 15: [mk_trans(0)],
                             17: [mk_trans(1)], 19: [mk_trans(2)],
                             21: [mk_trans(3)]}
                    tl_prev = tl
                    if pan < 3:
                        tail_hooks = hooks
                    else:
                        for kk in sorted(hooks):
                            for th in hooks[kk]:
                                th()
                        tl_prev = None

                kb.flush()

        finals = [(nm, kb.cnt[nm]) for nm in ("SOU", "SRES", "KD")
                  if kb.cnt[nm] > 0]
        for eng in ("sync", "scalar", "gpsimd"):
            kb.wait(eng, *finals)
        kb.flush()
    return nc, hc


_CACHED = None


def make_in_maps(hc, L0_attn_hidden, L0_attn_norm1_w, L0_attn_norm2_w,
                 L0_attn_g_wq, L0_attn_g_kv_down, L0_attn_g_k_up,
                 L0_attn_g_v_up, L0_attn_g_wo, L0_attn_l_wq, L0_attn_l_wk,
                 L0_attn_l_wv, L0_attn_l_wo, L0_ffn_w1, L0_ffn_b1, L0_ffn_w2,
                 L0_ffn_b2, cores=8):
    x = np.ascontiguousarray(np.asarray(L0_attn_hidden, dtype=np.float32))
    cmaps = {}
    for nm in ("ropeg", "ropel", "perm", "iden", "masks", "hsel", "sel16",
               "sel2", "onesM", "ones128", "onesk1b", "zrow", "cpool"):
        cmaps[nm] = np.ascontiguousarray(hc[nm])
    n1 = np.asarray(L0_attn_norm1_w, np.float32)
    n2 = np.asarray(L0_attn_norm2_w, np.float32)
    cmaps["n1t"] = np.ascontiguousarray(n1.reshape(8, 128).T)
    cmaps["n2t"] = np.ascontiguousarray(n2.reshape(8, 128).T)
    def tile_sq(w):
        return np.ascontiguousarray(
            np.asarray(w, np.float32).reshape(8, 128, 8, 128)
            .transpose(2, 1, 0, 3))

    def tile_kv(w, kc):
        a = np.asarray(w, np.float32)
        return np.ascontiguousarray(
            a.reshape(kc, 128, a.shape[1]).transpose(1, 0, 2))

    wmap = dict(gwq=tile_sq(L0_attn_g_wq), gkvd=tile_kv(L0_attn_g_kv_down, 8),
                gkup=tile_kv(L0_attn_g_k_up, 2), gvup=tile_kv(L0_attn_g_v_up, 2),
                gwo=tile_sq(L0_attn_g_wo), lwq=tile_sq(L0_attn_l_wq),
                lwk=tile_kv(L0_attn_l_wk, 8), lwv=tile_kv(L0_attn_l_wv, 8),
                lwo=tile_sq(L0_attn_l_wo))
    in_maps = []
    for c in range(cores):
        m = dict(cmaps)
        m["x_in"] = np.ascontiguousarray(x[2 * c:2 * c + 2])
        for k, v in wmap.items():
            m[k] = v
        m["w1c"] = np.ascontiguousarray(
            np.asarray(L0_ffn_w1[c], np.float32).reshape(8, 128, 32, 128)
            .transpose(2, 1, 0, 3))
        m["w2c"] = np.ascontiguousarray(
            np.asarray(L0_ffn_w2[c], np.float32).reshape(32, 128, 1024)
            .transpose(1, 0, 2))
        m["b1t"] = np.ascontiguousarray(
            np.asarray(L0_ffn_b1[c], np.float32).reshape(32, 128).T)
        m["b2t"] = np.ascontiguousarray(
            np.asarray(L0_ffn_b2[c], np.float32).reshape(8, 128).T)
        in_maps.append(m)
    return in_maps


def kernel(**inputs):
    global _CACHED
    if _CACHED is None:
        _CACHED = build_core_program()
    nc, hc = _CACHED
    in_maps = make_in_maps(hc, **inputs)
    res = run_bass_kernel_spmd(nc, in_maps, core_ids=list(range(8)))
    out = np.concatenate([r["y_out"] for r in res.results], axis=0)
    return np.ascontiguousarray(out.astype(np.float32))


if __name__ == "__main__":
    nc, _ = build_core_program()
    print("build ok; instructions:", nc.n_instructions())


# version 43
# speedup vs baseline: 1.0092x; 1.0066x over previous
import numpy as np
import ml_dtypes
from contextlib import ExitStack

import concourse.bass as bass
import concourse.mybir as mybir
from concourse.bass_utils import run_bass_kernel_spmd

F32 = mybir.dt.float32
BF16 = mybir.dt.bfloat16
AF = mybir.ActivationFunctionType
OP = mybir.AluOpType
BF = ml_dtypes.bfloat16

S = 2048
D = 1024
H = 16
KVH = 4
HD = 64
LAT = 256
WIN = 256
DFF = 4096
NSEQ = 2
NCH = 8
NTC = 16
C_EXP_SCALE = 0.125
C_EXP_BIAS = -8.0
C_INV_D = 1.0 / 1024.0
C_INV_HD = 0.015625
C_EPS_RMS = 1e-6
C_EPS_LN = 1e-5
C_NHALF = -0.5

ENGINES = ("sync", "tensor", "vector", "scalar", "gpsimd")


def _jax_rope_tables(base):
    import jax
    import jax.numpy as jnp
    with jax.default_device(jax.devices("cpu")[0]):
        inv_freq = 1.0 / base ** (jnp.arange(0, HD, 2, dtype=jnp.float32) / HD)
        freqs = jnp.arange(S, dtype=jnp.float32)[:, None] * inv_freq[None, :]
        cos, sin = jnp.cos(freqs), jnp.sin(freqs)
        return (np.asarray(cos, np.float32).T.copy(),
                np.asarray(sin, np.float32).T.copy())


def host_constants():
    c = {}
    for base, name in ((1e6, "ropeg"), (1e4, "ropel")):
        cos, sin = _jax_rope_tables(base)
        Ct = np.zeros((128, 2, S), np.float32)
        for p in range(128):
            hd = p % 64
            j = hd // 2
            Ct[p, 0] = cos[j]
            Ct[p, 1] = -sin[j] if hd % 2 == 0 else sin[j]
        c[name] = Ct
    perm = np.zeros((128, 128), BF)
    for m in range(128):
        perm[m ^ 1, m] = 1.0
    c["perm"] = perm
    c["iden"] = np.eye(128, dtype=np.float32)
    i = np.arange(128)
    mdiag = (i[None, :] >= i[:, None]).astype(BF)
    mlow = (i[None, :] < i[:, None]).astype(BF)
    c["masks"] = np.ascontiguousarray(np.stack([mdiag, mlow], axis=1))
    hsel = np.zeros((128, 8, 16), BF)
    for ch in range(8):
        hsel[0:64, ch, 2 * ch] = 1.0
        hsel[64:128, ch, 2 * ch + 1] = 1.0
    c["hsel"] = hsel
    sel16 = np.zeros((16, 8, 128), BF)
    for ch in range(8):
        sel16[2 * ch, ch, 0:64] = 1.0
        sel16[2 * ch + 1, ch, 64:128] = 1.0
    c["sel16"] = sel16
    sel2 = np.zeros((64, 128), BF)
    sel2[0, 0:64] = 1.0
    sel2[32, 64:128] = 1.0
    c["sel2"] = sel2
    c["onesM"] = np.ones((128, 32), BF)
    c["ones128"] = np.ones((128, 1), BF)
    c["onesk1b"] = np.ones((1, 128), BF)
    c["zrow"] = np.zeros((1, 576), BF)
    cp = np.zeros((128, 16), np.float32)
    vals = [C_EXP_SCALE, C_EXP_BIAS, C_INV_D, C_EPS_RMS, C_NHALF, C_INV_HD,
            C_EPS_LN, 1.0, 0.0, 0.5, -1.0]
    for j, v in enumerate(vals):
        cp[:, j] = v
    c["cpool"] = cp
    c["_cvals"] = vals
    return c


class KB:
    def __init__(self, nc, stack):
        self.nc = nc
        self.pend = []
        self.cnt = {}
        self.hwm = {}
        self.sems = {}
        self.bgate = [dict() for _ in range(8)]
        for name in ("LDC", "LS0", "LS1", "LW0", "LW1", "LX0", "LX1",
                     "LSF", "W2L0", "W2L1", "SOU", "SRES", "SX3", "KD",
                     "PE", "ACT", "DVE", "POOL", "SY"):
            self.sems[name] = stack.enter_context(nc.semaphore(name))
            self.cnt[name] = 0

    def op(self, eng, fn, waits=(), inc=None, amt=1):
        merged = {}
        for t in waits:
            if t is None:
                continue
            s, v = t
            if merged.get(s, 0) < v:
                merged[s] = v
        w2 = []
        for s, v in merged.items():
            key = (eng, s)
            if self.hwm.get(key, 0) < v:
                self.hwm[key] = v
                w2.append((s, v))
        tok = None
        if inc:
            self.cnt[inc] += amt
            tok = (inc, self.cnt[inc])
        self.pend.append((eng, fn, w2, (inc, amt) if inc else None))
        return tok

    def wait(self, eng, *toks):
        for t in toks:
            if t is None:
                continue
            self.op(eng, None, waits=(t,))

    def dma(self, eng, sem, out, in_, waits=()):
        return self.op(eng, lambda e, o=out, i=in_: e.dma_start(out=o, in_=i),
                       waits, inc=sem, amt=16)

    def bank_of(self, col):
        return col // 512

    def gates(self, c0, c1):
        out = {}
        for b in range(self.bank_of(c0), self.bank_of(c1 - 1) + 1):
            for s, v in self.bgate[b].items():
                if out.get(s, 0) < v:
                    out[s] = v
        return tuple(out.items())

    def read_psum(self, c0, c1, tok):
        if tok is None:
            return
        s, v = tok
        for b in range(self.bank_of(c0), self.bank_of(c1 - 1) + 1):
            if self.bgate[b].get(s, 0) < v:
                self.bgate[b][s] = v

    def barrier(self):
        toks = []
        ctr = {"sync": "SY", "tensor": "PE", "vector": "DVE",
               "scalar": "ACT", "gpsimd": "POOL"}
        for eng in ENGINES:
            toks.append(self.op(eng, lambda e: e.nop(), inc=ctr[eng]))
        for eng in ENGINES:
            self.wait(eng, *toks)
        for b in range(8):
            for t in toks:
                s, v = t
                if self.bgate[b].get(s, 0) < v:
                    self.bgate[b][s] = v

    def flush(self):
        pend = self.pend
        self.pend = []
        sems = self.sems
        with self.nc.Block() as blk:
            for eng in ENGINES:
                ops = [p for p in pend if p[0] == eng]
                if not ops:
                    continue

                def body(e, ops=ops):
                    for (_, fn, ws, inc) in ops:
                        for (s, v) in ws:
                            e.wait_ge(sems[s], v)
                        if fn is None:
                            continue
                        inst = fn(e)
                        if inc is not None:
                            inst.then_inc(sems[inc[0]], inc[1])
                getattr(blk, eng)(body)


def build_core_program(debug=False):
    nc = bass.Bass("TRN2", target_bir_lowering=False,
                   detect_race_conditions=False)
    hc = host_constants()

    x_in = nc.dram_tensor("x_in", [NSEQ, S, D], F32, kind="ExternalInput").ap()
    y_out = nc.dram_tensor("y_out", [NSEQ, S, D], F32, kind="ExternalOutput").ap()
    wd = {}
    for nm, shp in (("gwq", [8, 128, 8, 128]), ("gkvd", [128, 8, LAT]),
                    ("gkup", [128, 2, LAT]), ("gvup", [128, 2, LAT]),
                    ("gwo", [8, 128, 8, 128]), ("lwq", [8, 128, 8, 128]),
                    ("lwk", [128, 8, LAT]), ("lwv", [128, 8, LAT]),
                    ("lwo", [8, 128, 8, 128]), ("w1c", [32, 128, 8, 128]),
                    ("w2c", [128, 32, 1024])):
        wd[nm] = nc.dram_tensor(nm, shp, F32, kind="ExternalInput").ap()
    n1t_d = nc.dram_tensor("n1t", [128, 8], F32, kind="ExternalInput").ap()
    n2t_d = nc.dram_tensor("n2t", [128, 8], F32, kind="ExternalInput").ap()
    b1t_d = nc.dram_tensor("b1t", [128, 32], F32, kind="ExternalInput").ap()
    b2t_d = nc.dram_tensor("b2t", [128, 8], F32, kind="ExternalInput").ap()
    cd = {}
    for nm in ("ropeg", "ropel", "perm", "iden", "masks", "hsel", "sel16",
               "sel2", "onesM", "ones128", "onesk1b", "zrow", "cpool"):
        arr = hc[nm]
        dt = BF16 if arr.dtype == BF else F32
        cd[nm] = nc.dram_tensor(nm, list(arr.shape), dt, kind="ExternalInput").ap()

    if debug:
        dbg_t = nc.dram_tensor("dbg_t", [NCH, 128, S], F32, kind="ExternalOutput").ap()
        dbg_x3 = nc.dram_tensor("dbg_x3", [NCH, 128, S], F32, kind="ExternalOutput").ap()
        dbg_xn = nc.dram_tensor("dbg_xn", [NCH, 128, S], F32, kind="ExternalOutput").ap()
        dbg_rows = nc.dram_tensor("dbg_rows", [2, S], F32, kind="ExternalOutput").ap()
        dbg_q = nc.dram_tensor("dbg_q", [128, S], BF16, kind="ExternalOutput").ap()
        dbg_k = nc.dram_tensor("dbg_k", [128, KVH, S], BF16, kind="ExternalOutput").ap()
    scr0 = nc.dram_tensor("scr0", [NSEQ, NCH, 128, S], F32).ap()
    scrA = nc.dram_tensor("scrA", [NCH, 128, S], F32).ap()
    scrB = nc.dram_tensor("scrB", [NCH, 128, S], F32).ap()

    st = ExitStack()
    with st:
        kb = KB(nc, st)
        T = lambda *a: st.enter_context(nc.sbuf_tensor(*a))
        xnT = T("xnT", [128, NCH, S], BF16)
        r1 = T("r1", [1, 1024], F32)
        r2 = T("r2", [1, 1024], F32)
        r3 = T("r3", [1, 1024], F32)
        rh = T("rh", [1, 1024], BF16)
        rl = T("rl", [1, 1024], BF16)
        wstage = T("wstage", [128, S], F32)
        outstage = T("outstage", [128, 1024], F32)
        sqb = T("sqb", [128, 2 * S], BF16)
        cpool = T("cpool_s", [128, 16], F32)
        n1t = T("n1t_s", [128, 8], F32)
        n2t = T("n2t_s", [128, 8], F32)
        b1t = T("b1t_s", [128, 32], F32)
        b2t = T("b2t_s", [128, 8], F32)
        perm = T("perm_s", [128, 128], BF16)
        iden = T("iden_s", [128, 128], F32)
        masks = T("masks_s", [128, 2, 128], BF16)
        hsel = T("hsel_s", [128, 8, 16], BF16)
        sel16 = T("sel16_s", [16, 8, 128], BF16)
        sel2 = T("sel2_s", [64, 128], BF16)
        onesM = T("onesM_s", [128, 32], BF16)
        ones128 = T("ones128_s", [128, 1], BF16)
        onesk1b = T("onesk1b_s", [1, 128], BF16)
        zrow = T("zrow_s", [1, 576], BF16)
        A = st.enter_context(nc.psum_tensor("arena", [128, 4096], F32))

        for j, v in enumerate(hc["_cvals"]):
            nc.const_aps.register(F32, float(v), cpool[:, j:j + 1])

        def mm(out_cols, out_ap, lhsT, rhs, start, stop, waits=(), skip=False,
               tile_position=None):
            c0, c1 = out_cols
            w = tuple(waits) + kb.gates(c0, c1)
            return kb.op("tensor",
                         lambda e, o=out_ap, l=lhsT, r=rhs, st_=start, sp=stop,
                                tp=tile_position, sk=skip:
                         e.matmul(o, l, r, start=st_, stop=sp,
                                  tile_position=tp, skip_group_check=sk),
                         w, inc="PE")

        def trans(out_cols, out_ap, in_ap, waits=()):
            c0, c1 = out_cols
            w = tuple(waits) + kb.gates(c0, c1)
            return kb.op("tensor",
                         lambda e, o=out_ap, i=in_ap: e.transpose(o, i, iden[...]),
                         w, inc="PE")

        def dve(fn, waits=(), psum_read=None):
            t = kb.op("vector", fn, waits, inc="DVE")
            if psum_read:
                kb.read_psum(psum_read[0], psum_read[1], t)
            return t

        def act(fn, waits=(), psum_read=None):
            t = kb.op("scalar", fn, waits, inc="ACT")
            if psum_read:
                kb.read_psum(psum_read[0], psum_read[1], t)
            return t

        def pool(fn, waits=()):
            return kb.op("gpsimd", fn, waits, inc="POOL")

        ld = []
        for nm, dst in (("perm", perm), ("iden", iden), ("masks", masks),
                        ("hsel", hsel), ("sel16", sel16), ("sel2", sel2),
                        ("ones128", ones128), ("onesM", onesM),
                        ("onesk1b", onesk1b),
                        ("zrow", zrow), ("cpool", cpool)):
            ld.append(kb.dma("sync", "LDC", dst[...], cd[nm]))
        ld.append(kb.dma("sync", "LDC", n1t[...], n1t_d))
        ld.append(kb.dma("sync", "LDC", n2t[...], n2t_d))
        ld.append(kb.dma("sync", "LDC", b1t[...], b1t_d))
        ld.append(kb.dma("sync", "LDC", b2t[...], b2t_d))
        for eng in ("vector", "scalar", "tensor", "gpsimd"):
            kb.wait(eng, ld[-1])

        zl64 = zrow[0:1, 0:64]
        zl1 = zrow[0:1, 0:1]
        zl32 = zrow[0:1, 0:32]
        zr512 = zrow[0:1, 64:576]

        def cast_weight(dram, dst, nkc, ncols, fold=None, waits=(),
                        eng=None):
            eng = eng or dve
            t_l = kb.dma("sync", "LW0", wstage[:, 0:nkc * ncols]
                         .rearrange("p (kc n) -> p kc n", kc=nkc), dram,
                         waits=waits)
            toks = []
            for j in range(nkc):
                stg = wstage[:, j * ncols:(j + 1) * ncols]
                dst_ap = dst[:, j, 0:ncols]
                if fold is not None:
                    toks.append(eng(lambda e, d=dst_ap, s=stg,
                                    f=fold[:, j:j + 1]:
                                    e.tensor_scalar(d, s, f, None, OP.mult),
                                    waits=(t_l,)))
                else:
                    toks.append(eng(lambda e, d=dst_ap, s=stg:
                                    e.tensor_copy(d, s), waits=(t_l,)))
            return toks

        strmF_guard = [None, None]
        for s in range(NSEQ):
            ast = ExitStack()
            with ast:
                Ta = lambda n, *a: ast.enter_context(nc.sbuf_tensor(n + "_%d" % s, *a))
                k2 = Ta("k2", [128, KVH, S], BF16)
                v_all = Ta("v_all", [128, NTC, KVH, HD], BF16)
                qbf = Ta("qbf", [128, S], BF16)
                qrope = Ta("qrope", [128, S], F32)
                qfin = Ta("qfin", [128, S], BF16)
                pab = Ta("pab", [128, 6, 512], BF16)
                t1buf = pab[:, 0:4, :].rearrange("p a b -> p (a b)")
                OT = Ta("OT", [128, NCH, S], BF16)
                wqb = Ta("wqb", [128, 2, NCH, 128], BF16)
                wob = Ta("wob", [128, 2, NCH, 128], BF16)
                wkvb = Ta("wkvb", [128, NCH, LAT], BF16)
                wvb = Ta("wvb", [128, NCH, LAT], BF16)
                ropeC = Ta("ropeC", [128, S], BF16)
                ropeS = sqb[:, 2048:4096]
                rstd16f = Ta("rstd16f", [16, 1024], F32)
                rstd16b = Ta("rstd16b", [16, S], BF16)
                rcp = Ta("rcp", [64, 512], BF16)
                strm = [Ta("strm0", [128, S], F32), Ta("strm1", [128, S], F32)]
                latb = Ta("latb", [128, 2, S], BF16)
                lat = [latb[:, 0, :], latb[:, 1, :]]

                kb.barrier()

                ttr_last = [None, None]
                tcp_prev = [None, None]
                tsq_prev = None
                tst_prev = None
                tmm_prev = [None, None]
                st_toks = []
                for tc in range(NTC):
                    par = tc % 2
                    t_l = kb.dma("sync", "LS%d" % par, strm[par][:, 0:1024],
                                 x_in[s, tc * 128:(tc + 1) * 128, :],
                                 waits=(ttr_last[par],))
                    ttr = []
                    for k in range(NCH):
                        cc = par * 1024 + k * 128
                        ttr.append(trans((cc, cc + 128), A[:, cc:cc + 128],
                                         strm[par][:, k * 128:k * 128 + 128],
                                         waits=(t_l,)))
                    ttr_last[par] = ttr[-1]
                    t_cp = dve(lambda e, p=par: e.tensor_copy(
                        outstage[...], A[:, p * 1024:p * 1024 + 1024]),
                        waits=tuple(ttr) + (tsq_prev, tst_prev),
                        psum_read=(par * 1024, par * 1024 + 1024))
                    t_sq = act(lambda e, p=par: e.activation(
                        sqb[:, p * 1024:p * 1024 + 1024], outstage[...],
                        AF.Square), waits=(t_cp, tmm_prev[par]))
                    tsq_prev = t_sq
                    tm = []
                    for k in range(NCH):
                        tm.append(mm((2048 + tc * 128, 2048 + tc * 128 + 128),
                                     A[0:1, 2048 + tc * 128:2048 + tc * 128 + 128],
                                     ones128[...],
                                     sqb[:, par * 1024 + k * 128:
                                         par * 1024 + k * 128 + 128],
                                     start=(k == 0), stop=(k == NCH - 1),
                                     waits=(t_sq,)))
                    tmm_prev[par] = tm[-1]
                    t_st = kb.dma("scalar", "SOU",
                                  scr0[s][:, :, tc * 128:(tc + 1) * 128]
                                  .transpose([1, 0, 2]),
                                  outstage[...].rearrange("p (k t) -> p k t", k=8),
                                  waits=(t_cp,))
                    tst_prev = t_st
                    st_toks.append(t_st)
                x0_done = st_toks[-1]

                t_bcast = []
                for hh in range(2):
                    t_ln = act(lambda e, hh=hh: e.activation(
                        r1[...], A[0:1, 2048 + hh * 1024:3072 + hh * 1024],
                        AF.Ln, bias=C_EPS_RMS, scale=C_INV_D),
                        waits=(tmm_prev[0], tmm_prev[1]) + tuple(t_bcast),
                        psum_read=(2048 + hh * 1024, 3072 + hh * 1024))
                    t_rms = act(lambda e: e.activation(
                        r2[...], r1[...], AF.Exp, scale=C_NHALF),
                        waits=(t_ln,))
                    t_h = dve(lambda e: e.tensor_copy(rh[...], r2[...]),
                              waits=(t_rms,) + tuple(t_bcast))
                    t_lo = dve(lambda e: e.tensor_tensor(
                        rl[...], r2[...], rh[...], OP.subtract),
                        waits=(t_h,))
                    for p in range(2):
                        cc = hh * 1024 + p * 512
                        mm((cc, cc + 512), A[:, cc:cc + 512], onesk1b[...],
                           rh[:, p * 512:p * 512 + 512],
                           start=True, stop=False, waits=(t_lo,))
                        t_bcast.append(mm((cc, cc + 512), A[:, cc:cc + 512],
                                          onesk1b[...],
                                          rl[:, p * 512:p * 512 + 512],
                                          start=False, stop=True,
                                          waits=(t_lo,)))
                t_xn = [None] * NCH
                prevl = [None, None]
                for k in range(NCH):
                    par = k % 2
                    t_l = kb.dma("sync", "LS%d" % par, strm[par][...],
                                 scr0[s][k], waits=(prevl[par], x0_done))
                    t_xn[k] = dve(lambda e, kk=k, p=par: e.tensor_tensor(
                        xnT[:, kk, :], strm[p][...], A[:, 0:2048], OP.mult),
                        waits=(t_l,) + tuple(t_bcast), psum_read=(0, 2048))
                    prevl[par] = t_xn[k]
                t_xn_all = tuple(t_xn)

                for phase in range(2):
                    is_g = (phase == 0)
                    if not is_g:
                        kb.barrier()
                    foldv = n1t if is_g else n2t

                    tk_kv = cast_weight(wd["gkvd" if is_g else "lwk"], wkvb,
                                        8, LAT, fold=foldv)
                    if is_g:
                        tk_up = cast_weight(wd["gkup"], wvb, 2, LAT,
                                            waits=(tk_kv[-1],))
                        t_lv = kb.dma("sync", "LW0", wstage[:, 0:512]
                                      .rearrange("p (kc n) -> p kc n", kc=2),
                                      wd["gvup"], waits=(tk_up[-1],))
                        tk_vp = [dve(lambda e, j=j: e.tensor_copy(
                            wvb[:, 2 + j, :], wstage[:, j * 256:j * 256 + 256]),
                            waits=(t_lv,)) for j in range(2)]
                    else:
                        tk_vp = cast_weight(wd["lwv"], wvb, 8, LAT, fold=foldv,
                                            waits=(tk_kv[-1],))
                    rope_d = cd["ropeg" if is_g else "ropel"]
                    t_lr = kb.dma("sync", "LW0", wstage[...], rope_d[:, 0, :],
                                  waits=(tk_vp[-1],))
                    t_rc = dve(lambda e: e.tensor_copy(ropeC[...], wstage[...]),
                               waits=(t_lr,))
                    t_lr2 = kb.dma("sync", "LW0", wstage[...], rope_d[:, 1, :],
                                   waits=(t_rc,))
                    t_rs = dve(lambda e: e.tensor_copy(ropeS, wstage[...]),
                               waits=(t_lr2,))
                    wstage_free = t_rs

                    lat_toks = ()
                    if is_g:
                        lt = []
                        for m in range(2):
                            tmm = []
                            for k in range(NCH):
                                for p in range(4):
                                    tmm.append(mm(
                                        (p * 512, p * 512 + 512),
                                        A[:, p * 512:p * 512 + 512],
                                        wkvb[:, k, m * 128:m * 128 + 128],
                                        xnT[:, k, p * 512:p * 512 + 512],
                                        start=(k == 0), stop=(k == NCH - 1),
                                        waits=(tk_kv[k],) + t_xn_all))
                            lt.append(dve(lambda e, m=m: e.tensor_copy(
                                lat[m], A[:, 0:2048]), waits=tuple(tmm),
                                psum_read=(0, 2048)))
                        lat_toks = tuple(lt)

                    def rope_qknorm(chunk_idx, t_evac, dst_fn):
                        t_perm = []
                        for p in range(4):
                            t_perm.append(mm((p * 512, p * 512 + 512),
                                             A[:, p * 512:p * 512 + 512],
                                             perm[...],
                                             qbf[:, p * 512:p * 512 + 512],
                                             start=True, stop=True,
                                             waits=(t_evac, t_rc, t_rs)))
                        t_t1 = dve(lambda e: e.tensor_tensor(
                            t1buf, qbf[...], ropeC[...], OP.mult),
                            waits=(t_evac, t_rc))
                        t_r1 = dve(lambda e: e.tensor_tensor(
                            qrope[...], A[:, 0:2048], ropeS, OP.mult),
                            waits=tuple(t_perm) + (t_rs,),
                            psum_read=(0, 2048))
                        t_r2 = dve(lambda e: e.tensor_tensor(
                            qrope[...], qrope[...], t1buf, OP.add),
                            waits=(t_t1,))
                        t_sq = act(lambda e: e.activation(
                            sqb[:, 0:2048], qrope[...], AF.Square),
                            waits=(t_r2,))
                        t_ssq = []
                        for p in range(4):
                            t_ssq.append(mm(
                                (2048 + p * 512, 2048 + p * 512 + 512),
                                A[0:16, 2048 + p * 512:2048 + p * 512 + 512],
                                hsel[:, chunk_idx, :],
                                sqb[:, p * 512:p * 512 + 512],
                                start=True, stop=True, waits=(t_sq,)))
                        t_exq = None
                        for hh in range(2):
                            t_lnq = act(lambda e, hh=hh: e.activation(
                                rstd16f[...],
                                A[0:16, 2048 + hh * 1024:3072 + hh * 1024],
                                AF.Ln, bias=C_EPS_RMS, scale=C_INV_HD),
                                waits=tuple(t_ssq) +
                                ((t_exq,) if t_exq else ()),
                                psum_read=(2048 + hh * 1024, 3072 + hh * 1024))
                            t_exq = act(lambda e, hh=hh: e.activation(
                                rstd16b[:, hh * 1024:hh * 1024 + 1024],
                                rstd16f[...], AF.Exp, scale=C_NHALF),
                                waits=(t_lnq,))
                        t_bc = []
                        for p in range(4):
                            t_bc.append(mm((p * 512, p * 512 + 512),
                                           A[:, p * 512:p * 512 + 512],
                                           sel16[:, chunk_idx, :],
                                           rstd16b[:, p * 512:p * 512 + 512],
                                           start=True, stop=True,
                                           waits=(t_exq,)))
                        return dst_fn(t_bc, t_r2)

                    k2_toks = []
                    for m in range(2):
                        tmm = []
                        if is_g:
                            for kc in range(2):
                                for p in range(4):
                                    tmm.append(mm(
                                        (p * 512, p * 512 + 512),
                                        A[:, p * 512:p * 512 + 512],
                                        wvb[:, kc, m * 128:m * 128 + 128],
                                        lat[kc][:, p * 512:p * 512 + 512],
                                        start=(kc == 0), stop=(kc == 1),
                                        waits=lat_toks + (tk_up[kc],)))
                        else:
                            for k in range(NCH):
                                for p in range(4):
                                    tmm.append(mm(
                                        (p * 512, p * 512 + 512),
                                        A[:, p * 512:p * 512 + 512],
                                        wkvb[:, k, m * 128:m * 128 + 128],
                                        xnT[:, k, p * 512:p * 512 + 512],
                                        start=(k == 0), stop=(k == NCH - 1),
                                        waits=(tk_kv[k],) + t_xn_all))
                        t_evac = dve(lambda e: e.tensor_copy(qbf[...],
                                                             A[:, 0:2048]),
                                     waits=tuple(tmm), psum_read=(0, 2048))

                        def k_dst(t_bc, t_r2, m=m):
                            toks = []
                            for hloc in range(2):
                                g = 2 * m + hloc
                                r0 = hloc * 64
                                tt = dve(lambda e, g=g, r0=r0:
                                         e.tensor_tensor(
                                             k2[r0:r0 + 64, g, :],
                                             qrope[r0:r0 + 64, :],
                                             A[r0:r0 + 64, 0:2048],
                                             OP.mult),
                                         waits=tuple(t_bc) + (t_r2,),
                                         psum_read=(0, 2048))
                                d0 = 64 - r0
                                tcp = kb.dma("gpsimd", "KD",
                                             k2[d0:d0 + 64, g, :],
                                             k2[r0:r0 + 64, g, :],
                                             waits=(tt,))
                                toks.append(tcp)
                            return toks
                        k2_toks += rope_qknorm(m, t_evac, k_dst)
                    k2_all = tuple(k2_toks)

                    if debug and s == 0 and is_g:
                        kb.dma("gpsimd", "KD", dbg_k[...], k2[...],
                               waits=tuple(k2_toks))
                    wq_d = wd["gwq" if is_g else "lwq"]
                    wo_d = wd["gwo" if is_g else "lwo"]
                    wq_cast_tok = [wstage_free, wstage_free]
                    ot_toks = []
                    qfin_tok = [[None] * 4 for _ in range(NCH)]
                    tw_by_qc = [None] * NCH
                    rot = [0]
                    rot_mod = [3]

                    def take_slot():
                        w = rot[0] % rot_mod[0]
                        rot[0] += 1
                        return w

                    def wq_cast(qci, foldv=foldv):
                        par = qci % 2
                        t_lw = kb.dma("sync", "LW%d" % par,
                                      wstage[:, 1024 * par:1024 * par + 1024]
                                      .rearrange("p (kc n) -> p kc n", kc=8),
                                      wq_d[qci], waits=(wq_cast_tok[par],))
                        tw = []
                        for j in range(NCH):
                            tw.append(dve(lambda e, j=j, par=par,
                                          fv=foldv[:, j:j + 1]:
                                          e.tensor_scalar(
                                              wqb[:, par, j, :],
                                              wstage[:, 1024 * par + j * 128:
                                                      1024 * par + j * 128
                                                      + 128],
                                              fv, None, OP.mult),
                                          waits=(t_lw,)))
                        tw_by_qc[qci] = tw

                    def mk_chain(qci, pan):
                        lo = pan * 512
                        par = qci % 2
                        cs_ = {}

                        def th_qmm(lo=lo, par=par, qci=qci, cs_=cs_):
                            w = take_slot()
                            cs = w * 1024
                            tmm = []
                            for k in range(NCH):
                                tmm.append(mm(
                                    (cs, cs + 512), A[:, cs:cs + 512],
                                    wqb[:, par, k, :],
                                    xnT[:, k, lo:lo + 512],
                                    start=(k == 0), stop=(k == NCH - 1),
                                    waits=(tw_by_qc[qci][k],) + t_xn_all))
                            wq_cast_tok[par] = tmm[-1]
                            cs_["evac"] = dve(
                                lambda e, cs=cs, lo=lo: e.tensor_copy(
                                    qbf[:, lo:lo + 512], A[:, cs:cs + 512]),
                                waits=(tmm[-1],), psum_read=(cs, cs + 512))

                        def th_perm(lo=lo, cs_=cs_):
                            w = take_slot()
                            cs = w * 1024
                            t_p = mm((cs, cs + 512), A[:, cs:cs + 512],
                                     perm[...], qbf[:, lo:lo + 512],
                                     start=True, stop=True,
                                     waits=(cs_["evac"], t_rc, t_rs))
                            t_t1 = dve(lambda e, lo=lo: e.tensor_tensor(
                                sqb[:, lo:lo + 512], qbf[:, lo:lo + 512],
                                ropeC[:, lo:lo + 512], OP.mult),
                                waits=(cs_["evac"], t_rc))
                            t_r1 = dve(lambda e, cs=cs, lo=lo:
                                       e.tensor_tensor(
                                           qrope[:, lo:lo + 512],
                                           A[:, cs:cs + 512],
                                           ropeS[:, lo:lo + 512], OP.mult),
                                       waits=(t_p, t_rs),
                                       psum_read=(cs, cs + 512))
                            t_r2 = dve(lambda e, lo=lo: e.tensor_tensor(
                                qrope[:, lo:lo + 512], qrope[:, lo:lo + 512],
                                sqb[:, lo:lo + 512], OP.add),
                                waits=(t_t1, t_r1))
                            cs_["sq"] = dve(lambda e, lo=lo: e.tensor_tensor(
                                sqb[:, lo:lo + 512], qrope[:, lo:lo + 512],
                                qrope[:, lo:lo + 512], OP.mult),
                                waits=(t_r2,))

                        def th_ssq(lo=lo, qci=qci, cs_=cs_):
                            w = take_slot()
                            cs = w * 1024
                            t_s = mm((cs, cs + 512), A[0:16, cs:cs + 512],
                                     hsel[:, qci, :], sqb[:, lo:lo + 512],
                                     start=True, stop=True,
                                     waits=(cs_["sq"],))
                            t_ln = act(lambda e, cs=cs: e.activation(
                                rstd16f[:, 0:512], A[0:16, cs:cs + 512],
                                AF.Ln, bias=C_EPS_RMS, scale=C_INV_HD),
                                waits=(t_s,), psum_read=(cs, cs + 512))
                            cs_["exq"] = act(lambda e, lo=lo: e.activation(
                                rstd16b[:, lo:lo + 512], rstd16f[:, 0:512],
                                AF.Exp, scale=C_NHALF), waits=(t_ln,))

                        def th_bc(lo=lo, qci=qci, pan=pan, cs_=cs_):
                            w = take_slot()
                            cs = w * 1024
                            t_b = mm((cs, cs + 512), A[:, cs:cs + 512],
                                     sel16[:, qci, :],
                                     rstd16b[:, lo:lo + 512],
                                     start=True, stop=True,
                                     waits=(cs_["exq"],))
                            qfin_tok[qci][pan] = dve(
                                lambda e, cs=cs, lo=lo: e.tensor_tensor(
                                    qfin[:, lo:lo + 512],
                                    qrope[:, lo:lo + 512],
                                    A[:, cs:cs + 512], OP.mult),
                                waits=(t_b,), psum_read=(cs, cs + 512))

                        return [th_qmm, th_perm, th_ssq, th_bc]

                    wq_cast(0)
                    rot_mod[0] = 2
                    ch0 = []
                    for pan0 in range(4):
                        ch0 += mk_chain(0, pan0)
                    v_toks = []
                    for tc in range(NTC):
                        par = tc % 2
                        c0 = 2048 + par * 512
                        reg = A[:, c0:c0 + 256]
                        tmm = []
                        if is_g:
                            for kc in range(2):
                                tmm.append(mm((c0, c0 + 256), reg,
                                              lat[kc][:, tc * 128:tc * 128 + 128],
                                              wvb[:, 2 + kc, :],
                                              start=(kc == 0), stop=(kc == 1),
                                              waits=lat_toks + (tk_vp[-1],)))
                        else:
                            for k in range(NCH):
                                tmm.append(mm((c0, c0 + 256), reg,
                                              xnT[:, k, tc * 128:tc * 128 + 128],
                                              wvb[:, k, :],
                                              start=(k == 0), stop=(k == NCH - 1),
                                              waits=(tk_vp[k],) + t_xn_all))
                        v_toks.append(dve(lambda e, tc=tc, r=reg: e.tensor_copy(
                            v_all[:, tc, :, :], r), waits=tuple(tmm),
                            psum_read=(c0, c0 + 256)))
                        ch0[tc]()
                    v_all_tok = (v_toks[-1], v_toks[-2])
                    rot_mod[0] = 3

                    for qc in range(NCH):
                        par = qc % 2
                        g = qc // 2
                        if qc + 1 < NCH:
                            wq_cast(qc + 1)
                        if debug and s == 0 and is_g and qc == 0:
                            kb.dma("gpsimd", "KD", dbg_q[...], qfin[...],
                                   waits=tuple(qfin_tok[0]))

                        for pan in range(4):
                            lo = pan * 512
                            hi = lo + 512
                            t_qfin = qfin_tok[qc][pan]
                            ch = (mk_chain(qc + 1, pan)
                                  if qc + 1 < NCH else None)
                            if is_g:
                                chunks = list(range(0, hi // 128))
                            else:
                                chunks = [c for c in range(NTC)
                                          if c * 128 < hi
                                          and (c + 3) * 128 > lo]
                            av_zero = []
                            if not is_g:
                                av_zero.append(mm(
                                    (3072, 3584), A[0:64, 3072:3584],
                                    zl64, zr512, start=True, stop=True,
                                    skip=True, tile_position=(0, 0)))
                                av_zero.append(mm(
                                    (3072, 3584), A[64:128, 3072:3584],
                                    zl64, zr512, start=True, stop=True,
                                    skip=True, tile_position=(0, 64)))
                                av_zero.append(mm(
                                    (3584, 4096), A[0:32, 3584:4096],
                                    zl32, zr512, start=True, stop=True,
                                    skip=True, tile_position=(0, 0)))
                                av_zero.append(mm(
                                    (3584, 4096), A[32:64, 3584:4096],
                                    zl32, zr512, start=True, stop=True,
                                    skip=True, tile_position=(0, 32)))
                            pieces = []
                            for c in chunks:
                                a = max(c * 128, lo)
                                b = hi if is_g else min((c + 3) * 128, hi)
                                pieces.append((c, a, b))
                            av_toks = []
                            rs_toks = []
                            pfree = [None, None, None]
                            pend_av = [None, None, None]

                            piece_slot = {}

                            def emit_scores(i):
                                c, a, b = pieces[i]
                                wA = take_slot()
                                piece_slot[i] = wA
                                wdt = b - a
                                cs = wA * 1024
                                t_sa = mm((cs, cs + wdt), A[:, cs:cs + wdt],
                                          k2[0:64, g, c * 128:c * 128 + 128],
                                          qfin[0:64, a:b],
                                          start=True, stop=True,
                                          waits=(t_qfin,) + k2_all)
                                t_sb = mm((cs + 512, cs + 512 + wdt),
                                          A[:, cs + 512:cs + 512 + wdt],
                                          k2[64:128, g, c * 128:c * 128 + 128],
                                          qfin[64:128, a:b],
                                          start=True, stop=True,
                                          waits=(t_qfin,) + k2_all)
                                t_e = act(lambda e, wA=wA, wdt=wdt, cs=cs:
                                          e.activation(
                                              pab[:, 2 * wA:2 * wA + 2, 0:wdt],
                                              A[:, cs:cs + 1024]
                                              .rearrange("p (s w) -> p s w", s=2)
                                              [:, :, 0:wdt],
                                              AF.Exp, bias=C_EXP_BIAS,
                                              scale=C_EXP_SCALE),
                                          waits=(t_sa, t_sb, pfree[wA]),
                                          psum_read=(cs, cs + 1024))
                                mtoks = []
                                mask_blocks = ((c, 0),) if is_g                                     else ((c, 0), (c + 2, 1))
                                for (blk, mi) in mask_blocks:
                                    bstart = blk * 128
                                    if bstart < a or bstart >= b:
                                        continue
                                    off = bstart - a
                                    for hslot in range(2):
                                        mtoks.append(dve(
                                            lambda e, hs=hslot, wA=wA,
                                                   off=off, mi=mi:
                                            e.tensor_tensor(
                                                pab[:, 2 * wA + hs,
                                                    off:off + 128],
                                                pab[:, 2 * wA + hs,
                                                    off:off + 128],
                                                masks[:, mi, :], OP.mult),
                                            waits=(t_e,)))
                                pend_av[wA] = (i, tuple(mtoks) if mtoks
                                               else (t_e,))

                            def emit_av(wA):
                                if pend_av[wA] is None:
                                    return
                                i, avw0 = pend_av[wA]
                                pend_av[wA] = None
                                c, a, b = pieces[i]
                                wdt = b - a
                                la = a - lo
                                lb = b - lo
                                avw = avw0 + v_all_tok + tuple(av_zero)
                                fresh = is_g and (c == 0)
                                is_last_c = (c == chunks[-1])
                                t1 = mm((3072 + la, 3072 + lb),
                                        A[0:64, 3072 + la:3072 + lb],
                                        v_all[:, c, g, :],
                                        pab[:, 2 * wA, 0:wdt],
                                        start=fresh, stop=is_last_c,
                                        skip=True, waits=avw,
                                        tile_position=(0, 0))
                                t2 = mm((3072 + la, 3072 + lb),
                                        A[64:128, 3072 + la:3072 + lb],
                                        v_all[:, c, g, :],
                                        pab[:, 2 * wA + 1, 0:wdt],
                                        start=fresh, stop=is_last_c,
                                        skip=True, waits=avw,
                                        tile_position=(0, 64))
                                t3 = mm((3584 + la, 3584 + lb),
                                        A[0:32, 3584 + la:3584 + lb],
                                        onesM[...], pab[:, 2 * wA, 0:wdt],
                                        start=fresh, stop=is_last_c,
                                        skip=True, waits=avw,
                                        tile_position=(0, 0))
                                t4 = mm((3584 + la, 3584 + lb),
                                        A[32:64, 3584 + la:3584 + lb],
                                        onesM[...],
                                        pab[:, 2 * wA + 1, 0:wdt],
                                        start=fresh, stop=is_last_c,
                                        skip=True, waits=avw,
                                        tile_position=(0, 32))
                                av_toks.extend((t1, t2))
                                rs_toks.extend((t3, t4))
                                pfree[wA] = t4

                            n_p = len(pieces)
                            ch_pos = {0: 0, 1: 1, 3: 2}
                            for i in range(n_p):
                                if i >= 2:
                                    emit_av(piece_slot[i - 2])
                                emit_scores(i)
                                if ch is not None and i in ch_pos:
                                    ch[ch_pos[i]]()
                            emit_av(piece_slot[n_p - 2])
                            emit_av(piece_slot[n_p - 1])
                            if ch is not None:
                                ch[3]()

                            t_l = act(lambda e: e.activation(
                                rcp[...], A[0:64, 3584:4096], AF.Ln),
                                waits=tuple(rs_toks[-2:]),
                                psum_read=(3584, 4096))
                            t_r = act(lambda e: e.activation(
                                rcp[...], rcp[...], AF.Exp, scale=-1.0),
                                waits=(t_l,))
                            t_nb = mm((2560, 3072), A[:, 2560:3072],
                                      sel2[...], rcp[...], start=True,
                                      stop=True, waits=(t_r,))
                            t_cp = dve(lambda e, qc=qc, lo=lo: e.tensor_copy(
                                OT[:, qc, lo:lo + 512], A[:, 3072:3584]),
                                waits=tuple(av_toks[-2:]),
                                psum_read=(3072, 3584))
                            t_mul = dve(lambda e, qc=qc, lo=lo:
                                        e.tensor_tensor(
                                            OT[:, qc, lo:lo + 512],
                                            OT[:, qc, lo:lo + 512],
                                            A[:, 2560:3072], OP.mult),
                                        waits=(t_nb, t_cp),
                                        psum_read=(2560, 3072))
                            ot_toks.append(t_mul)

                    res_src = scr0[s] if is_g else scrA
                    res_dst = scrA if is_g else scrB
                    src_gate = x0_done if is_g else res_done_g
                    wo_cast_tok = [wq_cast_tok[0], wq_cast_tok[1]]
                    t_res_st = []
                    prev_add = None
                    prev_sq = None
                    prev_ld = [None, None]
                    red_toks = []
                    for m in range(NCH):
                        par = m % 2
                        t_lw = kb.dma("sync", "LW%d" % par,
                                      wstage[:, 1024 * par:1024 * par + 1024]
                                      .rearrange("p (kc n) -> p kc n", kc=8),
                                      wo_d[m], waits=(wo_cast_tok[par],))
                        tw = []
                        for j in range(NCH):
                            tw.append(act(lambda e, j=j, par=par:
                                          e.activation(
                                              wob[:, par, j, :],
                                              wstage[:, 1024 * par + j * 128:
                                                      1024 * par + j * 128
                                                      + 128],
                                              AF.Copy),
                                          waits=(t_lw,)))
                        wo_cast_tok[par] = tw[-1]
                        tmm = []
                        for k in range(NCH):
                            for p in range(4):
                                tmm.append(mm(
                                    (p * 512, p * 512 + 512),
                                    A[:, p * 512:p * 512 + 512],
                                    wob[:, par, k, :],
                                    OT[:, k, p * 512:p * 512 + 512],
                                    start=(k == 0), stop=(k == NCH - 1),
                                    waits=(tw[k], ot_toks[-1])))
                        sp = par if is_g else 0
                        t_l1 = kb.dma("sync", "LS%d" % sp, strm[sp][...],
                                      res_src[m], waits=(prev_ld[sp],
                                                         src_gate))
                        wr = (t_l1, prev_sq) + tuple(tmm)
                        if t_res_st:
                            wr = wr + (t_res_st[-1],)
                        t_l2 = None
                        if not is_g:
                            t_l2 = kb.dma("sync", "LS1", strm[1][...],
                                          scr0[s][m],
                                          waits=(prev_ld[1], x0_done))
                        t_a = dve(lambda e, sp=sp: e.tensor_tensor(
                            qrope[...], A[:, 0:2048], strm[sp][...],
                            OP.add), waits=wr, psum_read=(0, 2048))
                        prev_ld[sp] = t_a
                        if not is_g:
                            t_a = dve(lambda e: e.tensor_tensor(
                                qrope[...], qrope[...], strm[1][...],
                                OP.add), waits=(t_a, t_l2))
                            prev_ld[1] = t_a
                        prev_add = t_a
                        sb = (m % 2) * 2048
                        t_sq = act(lambda e, sb=sb: e.activation(
                            sqb[:, sb:sb + 2048], qrope[...], AF.Square),
                            waits=(t_a,))
                        prev_sq = t_sq
                        rr = []
                        for p in range(4):
                            rr.append(mm(
                                (2048 + p * 512, 2048 + p * 512 + 512),
                                A[0:1, 2048 + p * 512:2048 + p * 512 + 512],
                                ones128[...],
                                sqb[:, sb + p * 512:sb + p * 512 + 512],
                                start=(m == 0), stop=(m == NCH - 1),
                                skip=True, waits=(t_sq,),
                                tile_position=(0, 0)))
                        if not is_g:
                            t_cb = act(lambda e, par=par: e.activation(
                                latb[:, par, :], qrope[...], AF.Copy),
                                waits=(t_a,))
                            prev_add = t_cb
                            prev_sq = t_cb
                            for p in range(4):
                                rr.append(mm(
                                    (2048 + p * 512,
                                     2048 + p * 512 + 512),
                                    A[32:33,
                                      2048 + p * 512:2048 + p * 512 + 512],
                                    ones128[...],
                                    latb[:, par, p * 512:p * 512 + 512],
                                    start=(m == 0), stop=(m == NCH - 1),
                                    skip=True, waits=(t_cb,),
                                    tile_position=(0, 32)))
                        red_toks = rr
                        t_st = kb.dma("scalar", "SRES", res_dst[m],
                                      qrope[...], waits=(prev_add,))
                        t_res_st.append(t_st)
                    res_done = t_res_st[-1]
                    if is_g:
                        res_done_g = res_done
                        red_toks_g = red_toks
                    else:
                        red_toks_l = red_toks
                        res_st_l = list(t_res_st)

                    if is_g:
                        t_bcast = []
                        for hh in range(2):
                            t_ln = act(lambda e, hh=hh: e.activation(
                                r1[...],
                                A[0:1, 2048 + hh * 1024:3072 + hh * 1024],
                                AF.Ln, bias=C_EPS_RMS, scale=C_INV_D),
                                waits=tuple(red_toks_g) + tuple(t_bcast),
                                psum_read=(2048 + hh * 1024,
                                           3072 + hh * 1024))
                            t_rms = act(lambda e: e.activation(
                                r2[...], r1[...], AF.Exp, scale=C_NHALF),
                                waits=(t_ln,))
                            t_h = dve(lambda e: e.tensor_copy(
                                rh[...], r2[...]),
                                waits=(t_rms,) + tuple(t_bcast))
                            t_lo = dve(lambda e: e.tensor_tensor(
                                rl[...], r2[...], rh[...], OP.subtract),
                                waits=(t_h,))
                            for p in range(2):
                                cc = hh * 1024 + p * 512
                                mm((cc, cc + 512), A[:, cc:cc + 512],
                                   onesk1b[...],
                                   rh[:, p * 512:p * 512 + 512],
                                   start=True, stop=False, waits=(t_lo,))
                                t_bcast.append(mm((cc, cc + 512),
                                                  A[:, cc:cc + 512],
                                                  onesk1b[...],
                                                  rl[:, p * 512:p * 512 + 512],
                                                  start=False, stop=True,
                                                  waits=(t_lo,)))
                        t_xn = [None] * NCH
                        xstg = [strm[0][...], strm[1][...], qrope[...]]
                        prevl = [prev_ld[0], prev_ld[1],
                                 (prev_sq, t_res_st[-1])]
                        for k in range(NCH):
                            par = k % 3
                            if par == 2:
                                w0 = prevl[2] + (t_res_st[k],)
                                sem = "LX0"
                            else:
                                w0 = (prevl[par], t_res_st[k])
                                sem = "LS%d" % par
                            t_l = kb.dma("sync", sem,
                                         xstg[par], scrA[k], waits=w0)
                            t_xn[k] = dve(lambda e, kk=k, p=par:
                                          e.tensor_tensor(
                                              xnT[:, kk, :], xstg[p],
                                              A[:, 0:2048], OP.mult),
                                          waits=(t_l,) + tuple(t_bcast),
                                          psum_read=(0, 2048))
                            if par == 2:
                                prevl[2] = (t_xn[k],)
                            else:
                                prevl[par] = t_xn[k]
                        t_xn_all = tuple(t_xn)

                kb.flush()


            fst = ExitStack()
            with fst:
                Tf = lambda n, *a: fst.enter_context(nc.sbuf_tensor(n + "f%d" % s, *a))
                w2b = Tf("w2b", [128, 32, 1024], BF16)
                hT = Tf("hT", [128, 32, 512], BF16)
                w1mb = Tf("w1mb", [128, 2, NCH, 128], BF16)
                zpan = Tf("zpan", [128, NCH, 512], F32)
                zbf = hT[:, 24:32, :]
                f2048 = zpan[:, 0:4, :].rearrange("p a b -> p (a b)")
                strmF = [Tf("strmF0", [128, S], F32),
                         Tf("strmF1", [128, S], F32)]

                kb.barrier()

                t_stats = red_toks_l
                t_ab = []
                t_bb = []
                for hh in range(2):
                    h0 = hh * 1024
                    t_mean = act(lambda e, h0=h0: e.activation(
                        r1[...], A[32:33, 2048 + h0:2048 + h0 + 1024],
                        AF.Copy, scale=C_INV_D),
                        waits=tuple(t_stats) + tuple(t_ab) + tuple(t_bb),
                        psum_read=(2048 + h0, 2048 + h0 + 1024))
                    t_msq = act(lambda e: e.activation(
                        r2[...], r1[...], AF.Square), waits=(t_mean,))
                    t_v1 = dve(lambda e, h0=h0: e.tensor_scalar(
                        r3[...], A[0:1, 2048 + h0:2048 + h0 + 1024],
                        C_INV_D, None, OP.mult),
                        waits=tuple(t_stats) + tuple(t_bb),
                        psum_read=(2048 + h0, 2048 + h0 + 1024))
                    t_v2 = dve(lambda e: e.tensor_tensor(
                        r3[...], r3[...], r2[...], OP.subtract),
                        waits=(t_v1, t_msq))
                    t_lnr = act(lambda e: e.activation(
                        r2[...], r3[...], AF.Ln, bias=C_EPS_LN),
                        waits=(t_v2,))
                    t_a = act(lambda e: e.activation(
                        r2[...], r2[...], AF.Exp, scale=C_NHALF),
                        waits=(t_lnr,))
                    t_b1_ = dve(lambda e: e.tensor_tensor(
                        r3[...], r1[...], r2[...], OP.mult), waits=(t_a,))
                    t_b = dve(lambda e: e.tensor_scalar(
                        r3[...], r3[...], -1.0, None, OP.mult),
                        waits=(t_b1_,))
                    t_h = dve(lambda e: e.tensor_copy(rh[...], r2[...]),
                              waits=(t_a,) + tuple(t_ab))
                    t_lo = dve(lambda e: e.tensor_tensor(
                        rl[...], r2[...], rh[...], OP.subtract), waits=(t_h,))
                    for p in range(2):
                        cc = h0 + p * 512
                        mm((cc, cc + 512), A[:, cc:cc + 512], onesk1b[...],
                           rh[:, p * 512:p * 512 + 512],
                           start=True, stop=False, waits=(t_lo,))
                        t_ab.append(mm((cc, cc + 512), A[:, cc:cc + 512],
                                       onesk1b[...],
                                       rl[:, p * 512:p * 512 + 512],
                                       start=False, stop=True, waits=(t_lo,)))
                    t_h2 = dve(lambda e: e.tensor_copy(rh[...], r3[...]),
                               waits=(t_b,) + tuple(t_ab) + tuple(t_bb))
                    t_lo2 = dve(lambda e: e.tensor_tensor(
                        rl[...], r3[...], rh[...], OP.subtract), waits=(t_h2,))
                    for p in range(2):
                        cc = h0 + p * 512
                        mm((2048 + cc, 2048 + cc + 512),
                           A[:, 2048 + cc:2048 + cc + 512], onesk1b[...],
                           rh[:, p * 512:p * 512 + 512],
                           start=True, stop=False, waits=(t_lo2,))
                        t_bb.append(mm((2048 + cc, 2048 + cc + 512),
                                       A[:, 2048 + cc:2048 + cc + 512],
                                       onesk1b[...],
                                       rl[:, p * 512:p * 512 + 512],
                                       start=False, stop=True,
                                       waits=(t_lo2,)))
                zstg = zpan[:, 4:8, :].rearrange("p a b -> p (a b)")
                stg3 = [strmF[0][...], strmF[1][...], zstg]
                t_xnf = []
                prev_t1p = [None, None, None]
                for k in range(NCH):
                    par = k % 3
                    t_l = kb.dma("sync", "LSF", stg3[par], scrB[k],
                                 waits=(prev_t1p[par],
                                        strmF_guard[par] if par < 2 else None,
                                        res_st_l[k]))
                    t1 = dve(lambda e, p=par: e.tensor_tensor(
                        f2048, stg3[p], A[:, 0:2048], OP.mult),
                        waits=(t_l,) + tuple(t_ab) +
                        ((t_xnf[-1],) if t_xnf else ()),
                        psum_read=(0, 2048))
                    t2 = dve(lambda e: e.tensor_tensor(
                        f2048, f2048, A[:, 2048:4096], OP.add),
                        waits=tuple(t_bb), psum_read=(2048, 4096))
                    t3 = act(lambda e, kk=k: e.activation(
                        xnT[:, kk, :], f2048, AF.Copy), waits=(t2,))
                    t_xnf.append(t3)
                    prev_t1p[par] = t1
                ap_last = [prev_t1p[0], prev_t1p[1]]

                prev_cp2 = [None, None]
                prev_cpf = [strmF_guard[0], strmF_guard[1]]
                w2_toks = []
                for piece in range(32):
                    pr = piece % 2
                    if piece < 12:
                        stg = wstage[:, pr * 1024:pr * 1024 + 1024]
                        t_l = kb.dma("gpsimd", "W2L%d" % pr, stg,
                                     wd["w2c"][:, piece, :],
                                     waits=(prev_cp2[pr],))
                        tcp = pool(lambda e, pc=piece, stg=stg:
                                   e.tensor_copy(w2b[:, pc, :], stg),
                                   waits=(t_l,))
                        prev_cp2[pr] = tcp
                    else:
                        stg = strmF[pr][:, 0:1024]
                        t_l = kb.dma("gpsimd", "W2L%d" % pr, stg,
                                     wd["w2c"][:, piece, :],
                                     waits=(prev_cpf[pr], ap_last[pr]))
                        tcp = pool(lambda e, pc=piece, stg=stg:
                                   e.tensor_copy(w2b[:, pc, :], stg),
                                   waits=(t_l,))
                        prev_cpf[pr] = tcp
                    w2_toks.append(tcp)
                strmF_guard = [prev_cpf[0], prev_cpf[1]]

                w1_cast_tok = [w2_toks[10], w2_toks[11]]
                tail_hooks = {}
                tl_prev = None
                prev_cp = None
                prev_st = None

                for pan in range(4):
                    plo = pan * 512
                    gel_toks = []
                    for mdf in range(32):
                        for th in tail_hooks.pop(mdf, ()):
                            th()
                        par = mdf % 2
                        t_lw = kb.dma("sync", "LW%d" % par,
                                      wstage[:, 1024 * par:1024 * par + 1024]
                                      .rearrange("p (kc n) -> p kc n", kc=8),
                                      wd["w1c"][mdf], waits=(w1_cast_tok[par],))
                        t_c = dve(lambda e, par=par: e.tensor_copy(
                            w1mb[:, par, :, :],
                            wstage[:, 1024 * par:1024 * par + 1024]
                            .rearrange("p (kc n) -> p kc n", kc=8)),
                            waits=(t_lw,))
                        c0 = (mdf % 4) * 512
                        tmm = []
                        for k in range(NCH):
                            tmm.append(mm((c0, c0 + 512),
                                          A[:, c0:c0 + 512],
                                          w1mb[:, par, k, :],
                                          xnT[:, k, plo:plo + 512],
                                          start=(k == 0), stop=(k == NCH - 1),
                                          waits=(t_c, t_xnf[k])))
                        w1_cast_tok[par] = tmm[-1]
                        ggate = ((tl_prev["tsm"],)
                                 if (tl_prev is not None and mdf >= 24)
                                 else ())
                        gel_toks.append(act(
                            lambda e, mdf=mdf, c0=c0: e.activation(
                                hT[:, mdf, :], A[:, c0:c0 + 512],
                                AF.Gelu_apprx_tanh, bias=b1t[:, mdf:mdf + 1]),
                            waits=tuple(tmm) + ggate,
                            psum_read=(c0, c0 + 512)))
                    for kk in sorted(tail_hooks):
                        for th in tail_hooks.pop(kk):
                            th()
                    ztoks = []
                    for m in range(NCH):
                        tmmY = []
                        for k in range(32):
                            tmmY.append(mm((m * 512, m * 512 + 512),
                                           A[:, m * 512:m * 512 + 512],
                                           w2b[:, k, m * 128:m * 128 + 128],
                                           hT[:, k, :],
                                           start=(k == 0), stop=(k == 31),
                                           waits=(w2_toks[k],
                                                  gel_toks[k])))
                        t_e = dve(lambda e, m=m: e.tensor_scalar(
                            zpan[:, m, :], A[:, m * 512:m * 512 + 512],
                            b2t[:, m:m + 1], None, OP.add),
                            waits=(tmmY[-1],),
                            psum_read=(m * 512, m * 512 + 512))
                        t_z = dve(lambda e, m=m, plo=plo: e.tensor_tensor(
                            zpan[:, m, :], zpan[:, m, :],
                            xnT[:, m, plo:plo + 512],
                            OP.add), waits=(t_e, t_xnf[m]))
                        ztoks.append(t_z)
                    zb_h = [dve(lambda e, h=h: e.tensor_copy(
                        zbf[:, 4 * h:4 * h + 4, :],
                        zpan[:, 4 * h:4 * h + 4, :]),
                        waits=tuple(ztoks[4 * h:4 * h + 4]))
                        for h in range(2)]
                    zq_h = [act(lambda e, h=h: e.activation(
                        sqb[:, 2048 * h:2048 * h + 2048],
                        zpan[:, 4 * h:4 * h + 4, :]
                        .rearrange("p a b -> p (a b)"),
                        AF.Square), waits=tuple(ztoks[4 * h:4 * h + 4]))
                        for h in range(2)]
                    t_zb = zb_h
                    t_zq = zq_h

                    tl = {}

                    def th_reduce(t_zb=t_zb, t_zq=t_zq, tl=tl):
                        tsm = []
                        tsq = []
                        for m in range(NCH):
                            tsm.append(mm((2048, 2560), A[0:1, 2048:2560],
                                          ones128[...],
                                          zbf[:, m, :], start=(m == 0),
                                          stop=(m == NCH - 1),
                                          waits=(t_zb[m // 4],)))
                            tsq.append(mm((2560, 3072), A[0:1, 2560:3072],
                                          ones128[...],
                                          sqb[:, m * 512:m * 512 + 512],
                                          start=(m == 0),
                                          stop=(m == NCH - 1),
                                          waits=(t_zq[m // 4],)))
                        tl["tsm"] = tsm[-1]
                        tl["tsq"] = tsq[-1]

                    def th_chain(tl=tl):
                        t_mean = act(lambda e: e.activation(
                            r1[:, 0:512], A[0:1, 2048:2560], AF.Copy,
                            scale=C_INV_D), waits=(tl["tsm"],),
                            psum_read=(2048, 2560))
                        t_msq = act(lambda e: e.activation(
                            r2[:, 0:512], r1[:, 0:512], AF.Square),
                            waits=(t_mean,))
                        t_v1 = dve(lambda e: e.tensor_scalar(
                            r3[:, 0:512], A[0:1, 2560:3072], C_INV_D, None,
                            OP.mult), waits=(tl["tsq"],),
                            psum_read=(2560, 3072))
                        t_v2 = dve(lambda e: e.tensor_tensor(
                            r3[:, 0:512], r3[:, 0:512], r2[:, 0:512],
                            OP.subtract), waits=(t_v1, t_msq))
                        t_lnr = act(lambda e: e.activation(
                            r2[:, 0:512], r3[:, 0:512], AF.Ln,
                            bias=C_EPS_LN), waits=(t_v2,))
                        t_a = act(lambda e: e.activation(
                            r2[:, 0:512], r2[:, 0:512], AF.Exp,
                            scale=C_NHALF), waits=(t_lnr,))
                        t_b1_ = dve(lambda e: e.tensor_tensor(
                            r3[:, 0:512], r1[:, 0:512], r2[:, 0:512],
                            OP.mult), waits=(t_a,))
                        t_b = dve(lambda e: e.tensor_scalar(
                            r3[:, 0:512], r3[:, 0:512], -1.0,
                            None, OP.mult), waits=(t_b1_,))
                        t_h = dve(lambda e: e.tensor_copy(
                            rh[:, 0:512], r2[:, 0:512]), waits=(t_a,))
                        t_lo = dve(lambda e: e.tensor_tensor(
                            rl[:, 0:512], r2[:, 0:512], rh[:, 0:512],
                            OP.subtract), waits=(t_h,))
                        tl["t_lo"] = t_lo
                        tl["t_b"] = t_b

                    def th_bcast(tl=tl):
                        mm((2048, 2560), A[:, 2048:2560], onesk1b[...],
                           rh[:, 0:512], start=True, stop=False,
                           waits=(tl["t_lo"],))
                        t_abm = mm((2048, 2560), A[:, 2048:2560],
                                   onesk1b[...], rl[:, 0:512],
                                   start=False, stop=True,
                                   waits=(tl["t_lo"],))
                        t_h2 = dve(lambda e: e.tensor_copy(
                            rh[:, 512:1024], r3[:, 0:512]),
                            waits=(tl["t_b"], t_abm))
                        t_lo2 = dve(lambda e: e.tensor_tensor(
                            rl[:, 512:1024], r3[:, 0:512], rh[:, 512:1024],
                            OP.subtract), waits=(t_h2,))
                        mm((2560, 3072), A[:, 2560:3072], onesk1b[...],
                           rh[:, 512:1024], start=True, stop=False,
                           waits=(t_lo2,))
                        t_bbm = mm((2560, 3072), A[:, 2560:3072],
                                   onesk1b[...], rl[:, 512:1024],
                                   start=False, stop=True, waits=(t_lo2,))
                        tl["t_abm"] = t_abm
                        tl["t_bbm"] = t_bbm

                    def th_apply(t_zb=t_zb, t_zq=t_zq, tl=tl):
                        ap_toks = []
                        for m in range(NCH):
                            t1 = dve(lambda e, m=m: e.tensor_tensor(
                                zpan[:, m, :], zpan[:, m, :],
                                A[:, 2048:2560],
                                OP.mult),
                                waits=(tl["t_abm"], t_zb[1], t_zq[1]),
                                psum_read=(2048, 2560))
                            t2 = dve(lambda e, m=m: e.tensor_tensor(
                                zpan[:, m, :], zpan[:, m, :],
                                A[:, 2560:3072],
                                OP.add), waits=(tl["t_bbm"],),
                                psum_read=(2560, 3072))
                            ap_toks.append(t2)
                        tl["ap"] = tuple(ap_toks)

                    def mk_trans(tb, s=s, plo=plo, tl=tl):
                        def th_trans():
                            nonlocal prev_cp, prev_st
                            preg = 3072
                            ttr = []
                            for k in range(NCH):
                                cc = preg + k * 128
                                ttr.append(trans((cc, cc + 128),
                                                 A[:, cc:cc + 128],
                                                 zpan[:, k,
                                                      tb * 128:tb * 128 + 128],
                                                 waits=tl["ap"]))
                            t_cp = dve(lambda e, preg=preg: e.tensor_copy(
                                outstage[...], A[:, preg:preg + 1024]),
                                waits=tuple(ttr) + (prev_st,),
                                psum_read=(preg, preg + 1024))
                            t_stt = kb.dma("scalar", "SOU",
                                           y_out[s, plo + tb * 128:
                                                 plo + tb * 128 + 128, :],
                                           outstage[...], waits=(t_cp,))
                            prev_st = t_stt
                            prev_cp = t_cp
                        return th_trans

                    hooks = {1: [th_reduce], 3: [th_chain], 9: [th_bcast],
                             11: [th_apply], 15: [mk_trans(0)],
                             17: [mk_trans(1)], 19: [mk_trans(2)],
                             21: [mk_trans(3)]}
                    tl_prev = tl
                    if pan < 3:
                        tail_hooks = hooks
                    else:
                        for kk in sorted(hooks):
                            for th in hooks[kk]:
                                th()
                        tl_prev = None

                kb.flush()

        finals = [(nm, kb.cnt[nm]) for nm in ("SOU", "SRES", "KD")
                  if kb.cnt[nm] > 0]
        for eng in ("sync", "scalar", "gpsimd"):
            kb.wait(eng, *finals)
        kb.flush()
    return nc, hc


_CACHED = None


def make_in_maps(hc, L0_attn_hidden, L0_attn_norm1_w, L0_attn_norm2_w,
                 L0_attn_g_wq, L0_attn_g_kv_down, L0_attn_g_k_up,
                 L0_attn_g_v_up, L0_attn_g_wo, L0_attn_l_wq, L0_attn_l_wk,
                 L0_attn_l_wv, L0_attn_l_wo, L0_ffn_w1, L0_ffn_b1, L0_ffn_w2,
                 L0_ffn_b2, cores=8):
    x = np.ascontiguousarray(np.asarray(L0_attn_hidden, dtype=np.float32))
    cmaps = {}
    for nm in ("ropeg", "ropel", "perm", "iden", "masks", "hsel", "sel16",
               "sel2", "onesM", "ones128", "onesk1b", "zrow", "cpool"):
        cmaps[nm] = np.ascontiguousarray(hc[nm])
    n1 = np.asarray(L0_attn_norm1_w, np.float32)
    n2 = np.asarray(L0_attn_norm2_w, np.float32)
    cmaps["n1t"] = np.ascontiguousarray(n1.reshape(8, 128).T)
    cmaps["n2t"] = np.ascontiguousarray(n2.reshape(8, 128).T)
    def tile_sq(w):
        return np.ascontiguousarray(
            np.asarray(w, np.float32).reshape(8, 128, 8, 128)
            .transpose(2, 1, 0, 3))

    def tile_kv(w, kc):
        a = np.asarray(w, np.float32)
        return np.ascontiguousarray(
            a.reshape(kc, 128, a.shape[1]).transpose(1, 0, 2))

    wmap = dict(gwq=tile_sq(L0_attn_g_wq), gkvd=tile_kv(L0_attn_g_kv_down, 8),
                gkup=tile_kv(L0_attn_g_k_up, 2), gvup=tile_kv(L0_attn_g_v_up, 2),
                gwo=tile_sq(L0_attn_g_wo), lwq=tile_sq(L0_attn_l_wq),
                lwk=tile_kv(L0_attn_l_wk, 8), lwv=tile_kv(L0_attn_l_wv, 8),
                lwo=tile_sq(L0_attn_l_wo))
    in_maps = []
    for c in range(cores):
        m = dict(cmaps)
        m["x_in"] = np.ascontiguousarray(x[2 * c:2 * c + 2])
        for k, v in wmap.items():
            m[k] = v
        m["w1c"] = np.ascontiguousarray(
            np.asarray(L0_ffn_w1[c], np.float32).reshape(8, 128, 32, 128)
            .transpose(2, 1, 0, 3))
        m["w2c"] = np.ascontiguousarray(
            np.asarray(L0_ffn_w2[c], np.float32).reshape(32, 128, 1024)
            .transpose(1, 0, 2))
        m["b1t"] = np.ascontiguousarray(
            np.asarray(L0_ffn_b1[c], np.float32).reshape(32, 128).T)
        m["b2t"] = np.ascontiguousarray(
            np.asarray(L0_ffn_b2[c], np.float32).reshape(8, 128).T)
        in_maps.append(m)
    return in_maps


def kernel(**inputs):
    global _CACHED
    if _CACHED is None:
        _CACHED = build_core_program()
    nc, hc = _CACHED
    in_maps = make_in_maps(hc, **inputs)
    res = run_bass_kernel_spmd(nc, in_maps, core_ids=list(range(8)))
    out = np.concatenate([r["y_out"] for r in res.results], axis=0)
    return np.ascontiguousarray(out.astype(np.float32))


if __name__ == "__main__":
    nc, _ = build_core_program()
    print("build ok; instructions:", nc.n_instructions())
